# version 13
# baseline (speedup 1.0000x reference)
import contextlib

import numpy as np

import concourse.bass as bass
import concourse.mybir as mybir
from concourse.alu_op_type import AluOpType
from concourse.bass_isa import ReduceOp
from concourse.bass_utils import run_bass_kernel_spmd

F32 = mybir.dt.float32
BF16 = mybir.dt.bfloat16
AF = mybir.ActivationFunctionType
AX = mybir.AxisListType
OP = AluOpType

D = 4096
R = 1024
II = 11008
NT = 1024
TT = 512
QS = 256
DH = 128
NKV = 8
NQ = 32
GQ = 4
DC = D // 128
RC = R // 128
IC = II // 128
EPS = 1e-5
ATT_SCALE = DH ** -0.5

PREC = "bf16x3"
STOP_AFTER = None
DBG = False
AG_FAKE = False
ATTN_LVL = 6
N_CORES = 8
RN = R * NT
S_KEYS = N_CORES * TT
KC = S_KEYS // 128




_NAME_N = [0]


def _un(base):
    _NAME_N[0] += 1
    return f"{base}__{_NAME_N[0]}"


def SB(nc, name, shape, dt=F32):
    return nc.sbuf_tensor(_un(name), shape, dt)


def PS(nc, name, shape, dt=F32):
    return nc.psum_tensor(_un(name), shape, dt)


class Sem:
    def __init__(self, ctx, nc, name):
        self.name = name
        self.h = ctx.enter_context(nc.semaphore(name))
        self.v = 0
        self.kind = None


class Slot:
    __slots__ = ("w", "readers", "const", "waw", "sem")

    def __init__(self, const=False, waw=True):
        self.w = {}
        self.readers = []
        self.const = const
        self.waw = waw
        self.sem = None

    def pre_write(self):
        ws = list(self.readers)
        if self.waw:
            ws.extend(self.w.items())
        return ws

    def did_write(self, sem, val):
        if sem is not None:
            self.w[sem] = max(self.w.get(sem, 0), val)
        self.readers = []

    def pre_read(self):
        return list(self.w.items())

    def did_read(self, sem, val):
        if not self.const and sem is not None:
            self.readers.append((sem, val))


N_POOL = 64


class Prog:
    ENGS = ("sync", "vector", "scalar", "tensor", "gpsimd")
    OWN = {"vector": ("lo", 1), "scalar": ("hi", 1), "tensor": ("mm", 1),
           "gpsimd": ("gp", 1)}

    def __init__(self, nc, ctx):
        self.nc = nc
        self.sem = {n: Sem(ctx, nc, n) for n in ("hi", "lo", "mm", "gp", "cc")}
        for i in range(N_POOL):
            s = Sem(ctx, nc, f"q{i}")
            self.sem[s.name] = s
        pool = [self.sem[f"q{i}"] for i in range(N_POOL)]
        self.free = {"ld": pool[:44], "st": pool[44:]}
        self.pending = []
        self.dirty = set()
        self.ops = {e: [] for e in self.ENGS}
        self.seen = {e: {} for e in self.ENGS}
        self.nblocks = 0

    def acquire(self, kind):
        if not self.free[kind]:
            raise RuntimeError(f"sem pool exhausted ({kind})")
        s = self.free[kind].pop()
        s.kind = kind
        return s

    def release(self, *slots):
        for sl in slots:
            if sl.sem is not None:
                self.pending.append(sl.sem)
                sl.sem = None

    def rslot(self, stack, kind="ld", **kw):
        sl = Slot(**kw)
        sl.sem = self.acquire(kind)
        stack.callback(self.release, sl)
        return sl

    def v(self, name):
        return self.sem[name].v

    def _schedule(self, eng, fn, sem, inc, reads, writes, extra):
        waits = {}
        for s in reads:
            for nm, vv in s.pre_read():
                waits[nm] = max(waits.get(nm, 0), vv)
        for s in writes:
            for nm, vv in s.pre_write():
                waits[nm] = max(waits.get(nm, 0), vv)
        for nm, vv in extra:
            waits[nm] = max(waits.get(nm, 0), vv)
        seen = self.seen[eng]
        own_nm = self.OWN[eng][0] if eng in self.OWN else None
        emit_waits = []
        for nm, vv in waits.items():
            if vv <= 0 or seen.get(nm, 0) >= vv:
                continue
            if nm == own_nm and vv <= self.sem[nm].v:
                continue
            emit_waits.append((self.sem[nm].h, vv))
            seen[nm] = vv
        if sem is not None:
            sem.v += inc
            val = sem.v
            nm_out = sem.name
        else:
            val, nm_out = None, None
        self.ops[eng].append((fn, tuple(emit_waits),
                              (sem.h, inc) if sem is not None else None))
        for s in reads:
            s.did_read(nm_out, val)
        for s in writes:
            s.did_write(nm_out, val)
        return val

    def op(self, eng, fn, reads=(), writes=(), extra=()):
        nm, inc = self.OWN[eng]
        return self._schedule(eng, fn, self.sem[nm], inc, reads, writes, extra)

    def opn(self, eng, fn, reads=(), writes=(), extra=()):
        return self._schedule(eng, fn, None, 0, reads, writes, extra)

    def load(self, out_ap, in_ap, reads=(), writes=(), extra=()):
        dst = writes[0]
        if dst.sem is None:
            dst.sem = self.acquire("ld")
        self.dirty.add(dst.sem.name)
        fn = (lambda o, i: (lambda e: e.dma_start(out=o, in_=i)))(out_ap, in_ap)
        return self._schedule("sync", fn, dst.sem, 16, reads, writes, extra)

    def store(self, out_ap, in_ap, reads=(), writes=(), extra=()):
        src = reads[0]
        if src.sem is None:
            src.sem = self.acquire("st")
        self.dirty.add(src.sem.name)
        fn = (lambda o, i: (lambda e: e.dma_start(out=o, in_=i)))(out_ap, in_ap)
        return self._schedule("gpsimd", fn, src.sem, 16, reads, writes, extra)

    def collective(self, fn, reads=(), writes=()):
        return self._schedule("gpsimd", fn, self.sem["cc"], 1, reads, writes, ())

    def flush(self, name=None):
        name = _un(name or "b") + f"_{self.nblocks}"
        self.nblocks += 1
        drain = [("%s" % nm, self.sem[nm].v) for nm in sorted(self.dirty)]
        if drain:
            self._schedule("gpsimd", lambda e: e.nop(), None, 0, (), (), drain)
        with self.nc.Block(name) as blk:
            for eng in self.ENGS:
                ops = self.ops[eng]
                if not ops:
                    continue

                def body(e, ops=ops):
                    for fn, waits, inc in ops:
                        for h, vv in waits:
                            e.wait_ge(h, vv)
                        instr = fn(e)
                        if inc is not None:
                            instr.then_inc(inc[0], inc[1])

                getattr(blk, eng)(body)
        self.ops = {e: [] for e in self.ENGS}
        for e in self.ENGS:
            for nm in ("hi", "lo", "mm", "gp"):
                self.seen[e][nm] = self.sem[nm].v
            for nm, vv in drain:
                self.seen[e][nm] = vv
        self.dirty.clear()
        for s in self.pending:
            self.free[s.kind].append(s)
        self.pending = []


def npass():
    return 1 if PREC == "f32" else 3


def emit_group(P, psum_ap, psum_slot, lhs, rhs, lhs_slots, rhs_slots,
               first, last):
    if len(lhs) == 1 and len(rhs) == 1:
        combos = [(lhs[0], rhs[0], first, last)]
    else:
        combos = [(lhs[0], rhs[0], first, False),
                  (lhs[1], rhs[0], False, False),
                  (lhs[0], rhs[1], False, last)]
    v = None
    n = len(combos)
    for i, (a, b, st_, sp_) in enumerate(combos):
        fn = (lambda pa, aa, bb, s1, s2: (lambda e: e.matmul(
            pa, aa, bb, start=s1, stop=s2)))(psum_ap, a, b, st_, sp_)
        if i == 0 and n > 1:
            P.opn("tensor", fn,
                  reads=list(lhs_slots) + list(rhs_slots),
                  writes=[psum_slot] if first else ())
        elif i < n - 1:
            P.opn("tensor", fn)
        else:
            v = P.op("tensor", fn,
                     reads=(list(lhs_slots) + list(rhs_slots)) if n == 1 else (),
                     writes=([psum_slot] if first else ()) if n == 1 else ())
    psum_slot.did_write("mm", v)
    for s in list(lhs_slots) + list(rhs_slots):
        s.did_read("mm", v)
    return v


class WStream:

    def __init__(self, P, ctx, name, slots, width, split, dtype=F32, group=1):
        assert slots % group == 0
        self.P = P
        self.nslots = slots
        self.width = width
        self.group = group
        self.split = split and PREC == "bf16x3"
        ngrp = slots // group
        nc = P.nc
        self.raw = ctx.enter_context(SB(nc, f"{name}_r", [128, slots * width],
                                        dtype))
        self.raw_sl = [P.rslot(ctx, waw=False) for _ in range(ngrp)]
        if self.split:
            self.hib = ctx.enter_context(SB(nc, f"{name}_h",
                                            [128, slots * width], BF16))
            self.lob = ctx.enter_context(SB(nc, f"{name}_l",
                                            [128, slots * width], BF16))
            self.hi_sl = [Slot() for _ in range(ngrp)]
            self.lo_sl = [Slot() for _ in range(ngrp)]
        self.n = 0

    def load(self, in_ap, src_slot=None):
        P, i = self.P, self.n
        s = i % self.nslots
        gg = s // self.group
        srcs = [src_slot] if src_slot is not None else []
        c0 = s * self.width
        P.load(self.raw[:, c0:c0 + self.width],
               in_ap, reads=srcs, writes=[self.raw_sl[gg]])
        if (s % self.group) == self.group - 1 and self.split:
            a = gg * self.group * self.width
            b = (gg + 1) * self.group * self.width
            P.op("scalar",
                 (lambda o, aa: (lambda e: e.copy(out=o, in_=aa)))(
                     self.hib[:, a:b], self.raw[:, a:b]),
                 reads=[self.raw_sl[gg]], writes=[self.hi_sl[gg]])
            P.op("vector",
                 (lambda o, aa, bb: (lambda e: e.tensor_tensor(
                     out=o, in0=aa, in1=bb, op=OP.subtract)))(
                     self.lob[:, a:b], self.raw[:, a:b], self.hib[:, a:b]),
                 reads=[self.raw_sl[gg], self.hi_sl[gg]],
                 writes=[self.lo_sl[gg]])
        self.n += 1
        return i

    def bufs(self, i, c0, c1):
        s = i % self.nslots
        base = s * self.width
        if self.split:
            return (self.hib[:, base + c0:base + c1],
                    self.lob[:, base + c0:base + c1])
        return (self.raw[:, base + c0:base + c1],)

    def slots(self, i):
        gg = (i % self.nslots) // self.group
        if self.split:
            return (self.hi_sl[gg], self.lo_sl[gg])
        return (self.raw_sl[gg],)



def build_nc():
    nc = bass.Bass("TRN2", target_bir_lowering=False)
    ctx = contextlib.ExitStack()

    def din(name, shape):
        return nc.dram_tensor(name, shape, F32, kind="ExternalInput")

    x_in = din("x", [D, NT])
    u1 = din("u1", [D, R])
    v1 = din("v1", [R, 2 * II])
    u2 = din("u2", [II, R])
    v2 = din("v2", [R, D])
    b1r = din("b1r", [128, 2 * IC])
    b2r = din("b2r", [128, DC])
    bvb_d = din("bvb", [128, R])
    wq = din("wq", [D, D])
    wk = din("wk", [D, R])
    wv = din("wv", [D, R])
    wo = din("wo", [D, D])
    out_t = nc.dram_tensor("out", [D, NT], F32, kind="ExternalOutput")

    x1_t = nc.dram_tensor("x1_d", [D, NT], F32)
    x2_t = nc.dram_tensor("x2_d", [D, NT], F32)
    q_t = nc.dram_tensor("q_d", [D, NT], F32)
    if PREC == "f32":
        act_t = nc.dram_tensor("act_d", [IC, 128, NT], F32)
    else:
        act_t = nc.dram_tensor("act_d", [2, IC, 128, NT], BF16)
    kvs_t = nc.dram_tensor("kvs", [2 * RN], F32)
    kvf_t = nc.dram_tensor("kvf", [N_CORES * 2 * RN], F32)

    dsl = {n: Slot(waw=False) for n in ("x1", "x2", "q", "act", "kvs", "kvf")}

    with ctx:
        P = Prog(nc, ctx)

        def sb(name, shape, dt=F32):
            return ctx.enter_context(SB(nc, name, shape, dt))

        g = {}
        g["ones_kcol"] = sb("ones_kcol", [128, 1])
        g["ones_brow"] = sb("ones_brow", [1, 128])
        g["b1s"] = sb("b1s", [128, 2 * IC])
        g["b2h"] = sb("b2h", [128, DC])
        g["bvb"] = sb("bvb_s", [128, R])
        g["mb"] = sb("mb", [128, NT])
        g["rstdb"] = sb("rstdb", [128, NT])
        g["srow"] = sb("srow", [1, NT])
        g["srow2"] = sb("srow2", [1, NT])
        g["srow3"] = sb("srow3", [1, NT])
        gsl = {n: Slot() for n in g}
        gsl["ones_kcol"].const = True
        gsl["ones_brow"].const = True
        gsl["b1s"].const = True
        gsl["b2h"].const = True
        gsl["bvb"].const = True

        P.op("vector", lambda e: e.memset(g["ones_kcol"][:, :], 1.0),
             writes=[gsl["ones_kcol"]])
        P.op("vector", lambda e: e.memset(g["ones_brow"][:, :], 1.0),
             writes=[gsl["ones_brow"]])
        P.load(g["b1s"][:, :], b1r.ap()[:, :], writes=[gsl["b1s"]])
        P.load(g["b2h"][:, :], b2r.ap()[:, :], writes=[gsl["b2h"]])
        P.load(g["bvb"][:, :], bvb_d.ap()[:, :], writes=[gsl["bvb"]])
        P.op("vector", lambda e: e.tensor_scalar_mul(
            g["b2h"][:, :], g["b2h"][:, :], 0.5),
            reads=[gsl["b2h"]], writes=[gsl["b2h"]])
        gsl["b2h"].const = True

        emit_stats_pass(P, ctx, nc, x_in.ap(), None, g, gsl)
        P.flush("p0")

        if STOP_AFTER != "p0":
            emit_ffn(P, nc, x_src=x_in.ap(), x_src_slot=Slot(const=True),
                     x_dst=x1_t.ap(), x_dst_slot=dsl["x1"],
                     u1=u1.ap(), v1=v1.ap(), u2=u2.ap(), v2=v2.ap(),
                     act_t=act_t, act_slot=dsl["act"],
                     g=g, gsl=gsl, next_stats=True)

        if STOP_AFTER not in ("p0", "ffn1"):
            emit_qkv(P, nc, x1=x1_t.ap(), x1_slot=dsl["x1"],
                     wq=wq.ap(), wk=wk.ap(), wv=wv.ap(),
                     q_ap=q_t.ap(), q_slot=dsl["q"],
                     kvs=kvs_t, kvs_slot=dsl["kvs"],
                     kvf=kvf_t, kvf_slot=dsl["kvf"], g=g, gsl=gsl)

        if STOP_AFTER not in ("p0", "ffn1", "qkv", "ag"):
            emit_attention(P, nc, kvf=kvf_t, kvf_slot=dsl["kvf"],
                           q_ap=q_t.ap(), q_slot=dsl["q"],
                           wo=wo.ap(), x1=x1_t.ap(), x1_slot=dsl["x1"],
                           x2=x2_t.ap(), x2_slot=dsl["x2"], g=g, gsl=gsl)

        out_sl = Slot(waw=False)
        if STOP_AFTER is None:
            emit_ffn(P, nc, x_src=x2_t.ap(), x_src_slot=dsl["x2"],
                     x_dst=out_t.ap(), x_dst_slot=out_sl,
                     u1=u1.ap(), v1=v1.ap(), u2=u2.ap(), v2=v2.ap(),
                     act_t=act_t, act_slot=dsl["act"],
                     g=g, gsl=gsl, next_stats=False)

        if DBG:
            ranks = {"p0": 0, "ffn1": 1, "qkv": 2, "ag": 3, "attn": 4, None: 5}
            done = ranks[STOP_AFTER]
            dbg_specs = []
            if done >= 1:
                dbg_specs += [("dbg_x1", x1_t, dsl["x1"])]
                if PREC == "f32":
                    dbg_specs += [("dbg_act", act_t, dsl["act"])]
            if done >= 2:
                dbg_specs += [("dbg_q", q_t, dsl["q"]),
                              ("dbg_kvs", kvs_t, dsl["kvs"])]
            if done >= 3:
                dbg_specs += [("dbg_kvf", kvf_t, dsl["kvf"])]
            if done >= 4 and ATTN_LVL >= 6:
                dbg_specs += [("dbg_x2", x2_t, dsl["x2"])]
            for nm, srct, ssl in dbg_specs:
                dt_ = nc.dram_tensor(nm, list(srct.shape), F32,
                                     kind="ExternalOutput")
                P.store(dt_.ap(), srct.ap(), reads=[Slot(), ssl])
            with SB(nc, "dbg_sb", [128, 2 * NT]) as dsb:
                dbs = nc.dram_tensor("dbg_stats", [128, 2 * NT], F32,
                                     kind="ExternalOutput")
                sl = Slot()
                P.op("vector", lambda e: e.tensor_copy(
                    out=dsb[:, 0:NT], in_=g["mb"][:, :]),
                    reads=[gsl["mb"]], writes=[sl])
                P.op("vector", lambda e: e.tensor_copy(
                    out=dsb[:, NT:2 * NT], in_=g["rstdb"][:, :]),
                    reads=[gsl["rstdb"]], writes=[sl])
                P.store(dbs.ap(), dsb[:, :], reads=[sl])

        P.op("gpsimd", lambda e: e.nop(), extra=list(out_sl.w.items()))
        P.flush("fin")

    return nc



def emit_stats_pass(P, ctx, nc, x_ap, x_slot, g, gsl):
    with contextlib.ExitStack() as c:
        xc = [c.enter_context(SB(nc, f"st_xc{i}", [128, TT], F32))
              for i in range(4)]
        xsq = [c.enter_context(SB(nc, f"st_xq{i}", [128, TT], F32))
               for i in range(2)]
        xc_sl = [P.rslot(c, waw=False) for _ in range(4)]
        xsq_sl = [Slot() for _ in range(2)]
        ps = [c.enter_context(PS(nc, f"st_p{i}", [1, TT], F32))
              for i in range(4)]
        ps_sl = [Slot() for _ in range(4)]
        ps_b = c.enter_context(PS(nc, "st_b", [128, NT], F32))
        for t in range(2):
            for cch in range(DC):
                i = t * DC + cch
                xs, qs_ = i % 4, i % 2
                P.load(xc[xs][:, :],
                       x_ap[cch * 128:(cch + 1) * 128, t * TT:(t + 1) * TT],
                       reads=[x_slot] if x_slot else (), writes=[xc_sl[xs]])
                P.op("vector",
                     (lambda o, a: (lambda e: e.tensor_tensor(
                         out=o, in0=a, in1=a, op=OP.mult)))(
                         xsq[qs_][:, :], xc[xs][:, :]),
                     reads=[xc_sl[xs]], writes=[xsq_sl[qs_]])
                first, last = cch == 0, cch == DC - 1
                mm1 = (lambda pa, oa, ra, f_, l_: (lambda e: e.matmul(
                    pa, oa, ra, start=f_, stop=l_)))(
                    ps[2 * t][:, :], g["ones_kcol"][:, :], xc[xs][:, :],
                    first, last)
                P.op("tensor", mm1, reads=[gsl["ones_kcol"], xc_sl[xs]],
                     writes=[ps_sl[2 * t]] if first else ())
                if not first:
                    ps_sl[2 * t].did_write("mm", P.v("mm"))
                mm2 = (lambda pa, oa, ra, f_, l_: (lambda e: e.matmul(
                    pa, oa, ra, start=f_, stop=l_)))(
                    ps[2 * t + 1][:, :], g["ones_kcol"][:, :], xsq[qs_][:, :],
                    first, last)
                P.op("tensor", mm2, reads=[gsl["ones_kcol"], xsq_sl[qs_]],
                     writes=[ps_sl[2 * t + 1]] if first else ())
                if not first:
                    ps_sl[2 * t + 1].did_write("mm", P.v("mm"))
        finish_stats(P, [ps[0], ps[2]], [ps_sl[0], ps_sl[2]],
                     [ps[1], ps[3]], [ps_sl[1], ps_sl[3]], ps_b, g, gsl)


def finish_stats(P, ps_sums, ps_sums_sl, ps_ssqs, ps_ssqs_sl, ps_b, g, gsl):
    ps_b_sl = Slot()
    nT = len(ps_sums)
    for t in range(nT):
        cols = slice(t * TT, (t + 1) * TT)
        P.op("vector",
             (lambda o, a: (lambda e: e.tensor_scalar(
                 out=o, in0=a, scalar1=1.0 / D, scalar2=None, op0=OP.mult)))(
                 g["srow"][0:1, cols], ps_sums[t][:, :]),
             reads=[ps_sums_sl[t]], writes=[gsl["srow"]])
        P.op("vector",
             (lambda o, a: (lambda e: e.tensor_scalar(
                 out=o, in0=a, scalar1=1.0 / D, scalar2=EPS,
                 op0=OP.mult, op1=OP.add)))(
                 g["srow2"][0:1, cols], ps_ssqs[t][:, :]),
             reads=[ps_ssqs_sl[t]], writes=[gsl["srow2"]])
    nc_ = nT * TT
    P.op("vector", lambda e: e.tensor_tensor(
        out=g["srow3"][0:1, 0:nc_], in0=g["srow"][0:1, 0:nc_],
        in1=g["srow"][0:1, 0:nc_], op=OP.mult),
        reads=[gsl["srow"]], writes=[gsl["srow3"]])
    P.op("vector", lambda e: e.tensor_tensor(
        out=g["srow2"][0:1, 0:nc_], in0=g["srow2"][0:1, 0:nc_],
        in1=g["srow3"][0:1, 0:nc_], op=OP.subtract),
        reads=[gsl["srow2"], gsl["srow3"]], writes=[gsl["srow2"]])
    P.op("scalar", lambda e: e.activation(
        out=g["srow3"][0:1, 0:nc_], in_=g["srow2"][0:1, 0:nc_], func=AF.Sqrt),
        reads=[gsl["srow2"]], writes=[gsl["srow3"]])
    P.op("vector", lambda e: e.reciprocal(
        g["srow2"][0:1, 0:nc_], g["srow3"][0:1, 0:nc_]),
        reads=[gsl["srow3"]], writes=[gsl["srow2"]])
    for dst, dsl_, src, ssl in ((g["mb"], gsl["mb"], g["srow"], gsl["srow"]),
                                (g["rstdb"], gsl["rstdb"], g["srow2"], gsl["srow2"])):
        for t in range(nT):
            cols = slice(t * TT, (t + 1) * TT)
            P.op("tensor",
                 (lambda pa, oa, ra: (lambda e: e.matmul(
                     pa, oa, ra, start=True, stop=True)))(
                     ps_b[:, cols], g["ones_brow"][:, :], src[0:1, cols]),
                 reads=[gsl["ones_brow"], ssl], writes=[ps_b_sl])
            P.op("scalar",
                 (lambda o, a: (lambda e: e.copy(out=o, in_=a)))(
                     dst[:, cols], ps_b[:, cols]),
                 reads=[ps_b_sl], writes=[dsl_])



class YBufs:
    def __init__(self, P, ctx, nc, name):
        self.split = PREC == "bf16x3"
        if self.split:
            self.h = ctx.enter_context(SB(nc, f"{name}_h", [128, DC * TT], BF16))
            self.l = ctx.enter_context(SB(nc, f"{name}_l", [128, DC * TT], BF16))
        else:
            self.f = ctx.enter_context(SB(nc, f"{name}_f", [128, DC * TT], F32))
        self.sl = [Slot() for _ in range(DC)]

    def aps(self, c, a=0, b=TT):
        c0, c1 = c * TT + a, c * TT + b
        if self.split:
            return (self.h[:, c0:c1], self.l[:, c0:c1])
        return (self.f[:, c0:c1],)

    def slot(self, c):
        return (self.sl[c],)


class YScratch:
    def __init__(self, P, ctx, nc, name):
        self.xc = [ctx.enter_context(SB(nc, f"{name}_xc{i}", [128, TT], F32))
                   for i in range(4)]
        self.yt = [ctx.enter_context(SB(nc, f"{name}_yt{i}", [128, TT], F32))
                   for i in range(2)]
        self.xc_sl = [P.rslot(ctx, waw=False) for _ in range(4)]
        self.yt_sl = [Slot() for _ in range(2)]


def emit_build_y(P, nc, ys, x_ap, x_slot, t, yb, g, gsl):
    if True:
        xc, yt, xc_sl, yt_sl = ys.xc, ys.yt, ys.xc_sl, ys.yt_sl
        tcols = slice(t * TT, (t + 1) * TT)
        for cch in range(DC):
            xs, ts_ = cch % 4, cch % 2
            P.load(xc[xs][:, :],
                   x_ap[cch * 128:(cch + 1) * 128, tcols],
                   reads=[x_slot], writes=[xc_sl[xs]])
            P.op("vector",
                 (lambda o, a, b: (lambda e: e.tensor_tensor(
                     out=o, in0=a, in1=b, op=OP.subtract)))(
                     yt[ts_][:, :], xc[xs][:, :], g["mb"][:, tcols]),
                 reads=[xc_sl[xs], gsl["mb"]], writes=[yt_sl[ts_]])
            if not yb.split:
                P.op("vector",
                     (lambda o, a, b: (lambda e: e.tensor_tensor(
                         out=o, in0=a, in1=b, op=OP.mult)))(
                         yb.aps(cch)[0], yt[ts_][:, :], g["rstdb"][:, tcols]),
                     reads=[yt_sl[ts_], gsl["rstdb"]], writes=[yb.sl[cch]])
            else:
                P.op("vector",
                     (lambda o, a, b: (lambda e: e.tensor_tensor(
                         out=o, in0=a, in1=b, op=OP.mult)))(
                         yt[ts_][:, :], yt[ts_][:, :], g["rstdb"][:, tcols]),
                     reads=[yt_sl[ts_], gsl["rstdb"]], writes=[yt_sl[ts_]])
                P.op("scalar",
                     (lambda o, a: (lambda e: e.copy(out=o, in_=a)))(
                         yb.aps(cch)[0], yt[ts_][:, :]),
                     reads=[yt_sl[ts_]], writes=[yb.sl[cch]])
                P.op("vector",
                     (lambda o, a, b: (lambda e: e.tensor_tensor(
                         out=o, in0=a, in1=b, op=OP.subtract)))(
                         yb.aps(cch)[1], yt[ts_][:, :], yb.aps(cch)[0]),
                     reads=[yt_sl[ts_]], writes=[yb.sl[cch]])



def emit_ffn(P, nc, x_src, x_src_slot, x_dst, x_dst_slot, u1, v1, u2, v2,
             act_t, act_slot, g, gsl, next_stats):
    split = PREC == "bf16x3"
    with contextlib.ExitStack() as fctx:
        if split:
            pb_h = fctx.enter_context(SB(nc, "pf_h", [128, RC * NT], BF16))
            pb_l = fctx.enter_context(SB(nc, "pf_l", [128, RC * NT], BF16))
            pb = (pb_h, pb_l)
        else:
            pb = (fctx.enter_context(SB(nc, "pf", [128, RC * NT], F32)),)
        pb_sl = [Slot() for _ in range(RC * 2)]

        def pslc(r, t, a=0, b=TT):
            c0 = r * NT + t * TT + a
            c1 = r * NT + t * TT + b
            return tuple(x[:, c0:c1] for x in pb)

        with contextlib.ExitStack() as actx:
            yb = YBufs(P, actx, nc, "a_y")
            ys = YScratch(P, actx, nc, "a_ys")
            u1s = WStream(P, actx, "a_u1", 6, R, split=True, group=2)
            ps_p = [actx.enter_context(PS(nc, f"a_pp{r}", [128, TT], F32))
                    for r in range(RC)]
            ps_p_sl = [Slot() for _ in range(RC)]
            def a_load(cg):
                return [u1s.load(u1[(cg * 2 + j) * 128:
                                    (cg * 2 + j + 1) * 128, :])
                        for j in range(2)]

            for t in range(2):
                emit_build_y(P, nc, ys, x_src, x_src_slot, t, yb, g, gsl)
                pend_a = a_load(0)
                for cg in range(DC // 2):
                    wis = pend_a
                    if cg + 1 < DC // 2:
                        pend_a = a_load(cg + 1)
                    for j in range(2):
                        cch = cg * 2 + j
                        for r in range(RC):
                            emit_group(P, ps_p[r][:, :], ps_p_sl[r],
                                       u1s.bufs(wis[j], r * 128, (r + 1) * 128),
                                       yb.aps(cch),
                                       u1s.slots(wis[j]), yb.slot(cch),
                                       first=(cch == 0), last=(cch == DC - 1))
                for r in range(RC):
                    dst = pslc(r, t)
                    P.op("scalar",
                         (lambda o, a: (lambda e: e.copy(out=o, in_=a)))(
                             dst[0], ps_p[r][:, :]),
                         reads=[ps_p_sl[r]], writes=[pb_sl[r * 2 + t]])
                    if split:
                        P.op("vector",
                             (lambda o, a, b: (lambda e: e.tensor_tensor(
                                 out=o, in0=a, in1=b, op=OP.subtract)))(
                                 dst[1], ps_p[r][:, :], dst[0]),
                             reads=[ps_p_sl[r]], writes=[pb_sl[r * 2 + t]])
            if DBG and next_stats:
                dbg_pf = nc.dram_tensor("dbg_pf", [128, RC * NT], F32,
                                        kind="ExternalOutput")
                P.store(dbg_pf.ap(), pb[0][:, :],
                        reads=[pb_sl[i] for i in range(RC * 2)])
            P.flush("ffn_a")

        with contextlib.ExitStack() as bctx:
            v1s = WStream(P, bctx, "b_v1", 64, 128, split=True, group=16)
            silu = [bctx.enter_context(SB(nc, f"b_si{i}", [128, TT], F32))
                    for i in range(2)]
            actf = [bctx.enter_context(SB(nc, f"b_af{i}", [128, TT], F32))
                    for i in range(2)]
            silu_sl = [Slot() for _ in range(2)]
            actf_sl = [P.rslot(bctx, kind="st") for _ in range(2)]
            if split:
                acth = [bctx.enter_context(SB(nc, f"b_ah{i}", [128, TT], BF16))
                        for i in range(2)]
                actl = [bctx.enter_context(SB(nc, f"b_al{i}", [128, TT], BF16))
                        for i in range(2)]
                acth_sl = [P.rslot(bctx, kind="st") for _ in range(2)]
                actl_sl = [P.rslot(bctx, kind="st") for _ in range(2)]
            ps_g = [bctx.enter_context(PS(nc, f"b_pg{i}", [128, TT], F32))
                    for i in range(4)]
            ps_u = [bctx.enter_context(PS(nc, f"b_pu{i}", [128, TT], F32))
                    for i in range(4)]
            ps_g_sl = [Slot() for _ in range(4)]
            ps_u_sl = [Slot() for _ in range(4)]
            def b_load(f):
                wg = [v1s.load(v1[r * 128:(r + 1) * 128, f * 128:(f + 1) * 128])
                      for r in range(RC)]
                wu = [v1s.load(v1[r * 128:(r + 1) * 128,
                                  (IC + f) * 128:(IC + f + 1) * 128])
                      for r in range(RC)]
                return wg, wu

            pend_b = b_load(0)
            for f in range(IC):
                wg, wu = pend_b
                if f + 1 < IC:
                    pend_b = b_load(f + 1)
                for t in range(2):
                    pbk = (f % 2) * 2 + t
                    for r in range(RC):
                        emit_group(P, ps_g[pbk][:, :], ps_g_sl[pbk],
                                   v1s.bufs(wg[r], 0, 128), pslc(r, t),
                                   v1s.slots(wg[r]), [pb_sl[r * 2 + t]],
                                   first=(r == 0), last=(r == RC - 1))
                        emit_group(P, ps_u[pbk][:, :], ps_u_sl[pbk],
                                   v1s.bufs(wu[r], 0, 128), pslc(r, t),
                                   v1s.slots(wu[r]), [pb_sl[r * 2 + t]],
                                   first=(r == 0), last=(r == RC - 1))
                    si = t % 2
                    P.op("scalar",
                         (lambda o, a, bias, pg: (lambda e: e.activation(
                             out=o, in_=a, func=AF.Silu, bias=bias, scale=1.0)))(
                             silu[si][:, :], ps_g[pbk][:, :], g["b1s"][:, f:f + 1],
                             pbk),
                         reads=[ps_g_sl[pbk], gsl["b1s"]], writes=[silu_sl[si]])
                    P.op("vector",
                         (lambda o, a, sc, b: (lambda e: e.scalar_tensor_tensor(
                             out=o, in0=a, scalar=sc, in1=b,
                             op0=OP.add, op1=OP.mult)))(
                             actf[si][:, :], ps_u[pbk][:, :],
                             g["b1s"][:, IC + f:IC + f + 1], silu[si][:, :]),
                         reads=[ps_u_sl[pbk], gsl["b1s"], silu_sl[si]],
                         writes=[actf_sl[si]])
                    if split:
                        P.op("scalar",
                             (lambda o, a: (lambda e: e.copy(out=o, in_=a)))(
                                 acth[si][:, :], actf[si][:, :]),
                             reads=[actf_sl[si]], writes=[acth_sl[si]])
                        P.op("vector",
                             (lambda o, a, b: (lambda e: e.tensor_tensor(
                                 out=o, in0=a, in1=b, op=OP.subtract)))(
                                 actl[si][:, :], actf[si][:, :], acth[si][:, :]),
                             reads=[actf_sl[si], acth_sl[si]], writes=[actl_sl[si]])
                        P.store(act_t.ap()[0, f, :, t * TT:(t + 1) * TT],
                                acth[si][:, :], reads=[acth_sl[si]],
                                writes=[act_slot])
                        P.store(act_t.ap()[1, f, :, t * TT:(t + 1) * TT],
                                actl[si][:, :], reads=[actl_sl[si]],
                                writes=[act_slot])
                    else:
                        P.store(act_t.ap()[f, :, t * TT:(t + 1) * TT],
                                actf[si][:, :], reads=[actf_sl[si]],
                                writes=[act_slot])
            P.flush("ffn_b")

        if split:
            zb = (fctx.enter_context(SB(nc, "zf_h", [128, RC * NT], BF16)),
                  fctx.enter_context(SB(nc, "zf_l", [128, RC * NT], BF16)))
        else:
            zb = (fctx.enter_context(SB(nc, "zf", [128, RC * NT], F32)),)
        zb_sl = [Slot() for _ in range(RC * 2)]

        def zslc(r, t):
            c0, c1 = r * NT + t * TT, r * NT + (t + 1) * TT
            return tuple(x[:, c0:c1] for x in zb)

        with contextlib.ExitStack() as cctx:
            u2s = WStream(P, cctx, "c_u2", 8, R, split=True, group=2)
            adt = F32 if PREC == "f32" else BF16
            acs = WStream(P, cctx, "c_ac", 8, TT, split=False, dtype=adt,
                          group=2)
            acl = (WStream(P, cctx, "c_al", 8, TT, split=False, dtype=BF16,
                           group=2)
                   if split else None)
            ps_z = [cctx.enter_context(PS(nc, f"c_pz{r}", [128, TT], F32))
                    for r in range(RC)]
            ps_z_sl = [Slot() for _ in range(RC)]
            for t in range(2):
                def c_load(fg, t=t):
                    fll = []
                    for j in range(2):
                        f = fg * 2 + j
                        wi = u2s.load(u2[f * 128:(f + 1) * 128, :])
                        if split:
                            ai = acs.load(act_t.ap()[0, f, :,
                                                     t * TT:(t + 1) * TT],
                                          src_slot=act_slot)
                            ali = acl.load(act_t.ap()[1, f, :,
                                                      t * TT:(t + 1) * TT],
                                           src_slot=act_slot)
                        else:
                            ai = acs.load(act_t.ap()[f, :, t * TT:(t + 1) * TT],
                                          src_slot=act_slot)
                            ali = None
                        fll.append((f, wi, ai, ali))
                    return fll

                pend_c = c_load(0)
                for fg in range(IC // 2):
                    fll = pend_c
                    if fg + 1 < IC // 2:
                        pend_c = c_load(fg + 1)
                    for f, wi, ai, ali in fll:
                        if split:
                            rhs = (acs.bufs(ai, 0, TT)[0],
                                   acl.bufs(ali, 0, TT)[0])
                            rsl = [acs.slots(ai)[0], acl.slots(ali)[0]]
                        else:
                            rhs = (acs.bufs(ai, 0, TT)[0],)
                            rsl = [acs.slots(ai)[0]]
                        for r in range(RC):
                            emit_group(P, ps_z[r][:, :], ps_z_sl[r],
                                       u2s.bufs(wi, r * 128, (r + 1) * 128),
                                       rhs, u2s.slots(wi), rsl,
                                       first=(f == 0), last=(f == IC - 1))
                for r in range(RC):
                    dst = zslc(r, t)
                    P.op("scalar",
                         (lambda o, a: (lambda e: e.copy(out=o, in_=a)))(
                             dst[0], ps_z[r][:, :]),
                         reads=[ps_z_sl[r]], writes=[zb_sl[r * 2 + t]])
                    if split:
                        P.op("vector",
                             (lambda o, a, b: (lambda e: e.tensor_tensor(
                                 out=o, in0=a, in1=b, op=OP.subtract)))(
                                 dst[1], ps_z[r][:, :], dst[0]),
                             reads=[ps_z_sl[r]], writes=[zb_sl[r * 2 + t]])
                if DBG and next_stats and t == 1:
                    dbg_zf = nc.dram_tensor("dbg_zf", [128, RC * NT], F32,
                                            kind="ExternalOutput")
                    P.store(dbg_zf.ap(), zb[0][:, :],
                            reads=[zb_sl[i] for i in range(RC * 2)])
                P.flush(f"ffn_c{t}")

        if next_stats:
            ps_st = [fctx.enter_context(PS(nc, f"d_ps{i}", [1, TT], F32))
                     for i in range(4)]
            ps_st_sl = [Slot() for _ in range(4)]
        with contextlib.ExitStack() as dctx:
            v2s = WStream(P, dctx, "d_v2", 16, TT, split=True, group=4)
            xc2 = [dctx.enter_context(SB(nc, f"d_xc{i}", [128, TT], F32))
                   for i in range(2)]
            xo = [dctx.enter_context(SB(nc, f"d_xo{i}", [128, TT], F32))
                  for i in range(2)]
            tb = [dctx.enter_context(SB(nc, f"d_tb{i}", [128, TT], F32))
                  for i in range(2)]
            xc2_sl = [P.rslot(dctx, waw=False) for _ in range(2)]
            xo_sl = [P.rslot(dctx, kind="st") for _ in range(2)]
            tb_sl = [Slot() for _ in range(2)]
            ps_o = [dctx.enter_context(PS(nc, f"d_po{i}", [128, TT], F32))
                    for i in range(4)]
            ps_o_sl = [Slot() for _ in range(4)]
            dseq = [(dog, rg) for dog in range(8) for rg in range(2)]

            def d_load(idx, t):
                dog, rg = dseq[idx]
                return [v2s.load(v2[(rg * 4 + k) * 128:
                                    (rg * 4 + k + 1) * 128,
                                    dog * 512:(dog + 1) * 512])
                        for k in range(4)]

            for t in range(2):
                pend_d = d_load(0, t)
                for idx, (dog, rg) in enumerate(dseq):
                    wis = pend_d
                    if idx + 1 < len(dseq):
                        pend_d = d_load(idx + 1, t)
                    for k in range(4):
                        r = rg * 4 + k
                        for do in range(4):
                            emit_group(P, ps_o[do][:, :], ps_o_sl[do],
                                       v2s.bufs(wis[k], do * 128,
                                                (do + 1) * 128),
                                       zslc(r, t),
                                       v2s.slots(wis[k]),
                                       [zb_sl[r * 2 + t]],
                                       first=(r == 0), last=(r == RC - 1))
                    if rg != 1:
                        continue
                    for do in range(4):
                        dd = dog * 4 + do
                        xs = dd % 2
                        P.load(xc2[xs][:, :],
                               x_src[dd * 128:(dd + 1) * 128, t * TT:(t + 1) * TT],
                               reads=[x_src_slot], writes=[xc2_sl[xs]])
                        P.op("scalar",
                             (lambda o, a, bias: (lambda e: e.activation(
                                 out=o, in_=a, func=AF.Identity,
                                 bias=bias, scale=0.5)))(
                                 tb[xs][:, :], ps_o[do][:, :], g["b2h"][:, dd:dd + 1]),
                             reads=[ps_o_sl[do], gsl["b2h"]], writes=[tb_sl[xs]])
                        P.op("vector",
                             (lambda o, a, b: (lambda e: e.tensor_tensor(
                                 out=o, in0=a, in1=b, op=OP.add)))(
                                 xo[xs][:, :], tb[xs][:, :], xc2[xs][:, :]),
                             reads=[tb_sl[xs], xc2_sl[xs]], writes=[xo_sl[xs]])
                        P.store(x_dst[dd * 128:(dd + 1) * 128, t * TT:(t + 1) * TT],
                                xo[xs][:, :], reads=[xo_sl[xs]],
                                writes=[x_dst_slot])
                        if next_stats:
                            P.op("vector",
                                 (lambda o, a: (lambda e: e.tensor_tensor(
                                     out=o, in0=a, in1=a, op=OP.mult)))(
                                     tb[xs][:, :], xo[xs][:, :]),
                                 reads=[xo_sl[xs]], writes=[tb_sl[xs]])
                            first = (dog == 0 and do == 0)
                            last = (dog == 7 and do == 3)
                            emit_group(P, ps_st[2 * t][:, :], ps_st_sl[2 * t],
                                       (g["ones_kcol"][:, :],), (xo[xs][:, :],),
                                       [gsl["ones_kcol"]], [xo_sl[xs]],
                                       first=first, last=last)
                            emit_group(P, ps_st[2 * t + 1][:, :], ps_st_sl[2 * t + 1],
                                       (g["ones_kcol"][:, :],), (tb[xs][:, :],),
                                       [gsl["ones_kcol"]], [tb_sl[xs]],
                                       first=first, last=last)
            P.flush("ffn_d")
        if next_stats:
            with PS(nc, "d_bc", [128, NT], F32) as ps_b:
                finish_stats(P, [ps_st[0], ps_st[2]], [ps_st_sl[0], ps_st_sl[2]],
                             [ps_st[1], ps_st[3]], [ps_st_sl[1], ps_st_sl[3]],
                             ps_b, g, gsl)
                P.flush("ffn_dst")



def emit_qkv(P, nc, x1, x1_slot, wq, wk, wv, q_ap, q_slot, kvs, kvs_slot,
             kvf, kvf_slot, g, gsl):
    split = PREC == "bf16x3"

    def ag(src_off, n, dst_off):
        P.collective(
            (lambda so, nn, do: (lambda e: e.collective_compute(
                "AllGather", OP.bypass,
                replica_groups=[list(range(N_CORES))],
                ins=[bass.AP(kvs, so, [[1, nn]]).opt()],
                outs=[bass.AP(kvf, do, [[1, N_CORES * nn]]).opt()])))(
                    src_off, n, dst_off),
            reads=[kvs_slot], writes=[kvf_slot])
    with contextlib.ExitStack() as ectx:
        yb = YBufs(P, ectx, nc, "e_y")
        ys = YScratch(P, ectx, nc, "e_ys")
        ws = WStream(P, ectx, "e_w", 12, TT, split=True, group=4)
        qout = [ectx.enter_context(SB(nc, f"e_qo{i}", [128, TT], F32))
                for i in range(2)]
        qout_sl = [P.rslot(ectx, kind="st") for _ in range(2)]
        for t in range(2):
            emit_build_y(P, nc, ys, x1, x1_slot, t, yb, g, gsl)
            with contextlib.ExitStack() as qctx:
                ps_q = [qctx.enter_context(PS(nc, f"e_pq{t}_{i}", [128, TT], F32))
                        for i in range(4)]
                ps_q_sl = [Slot() for _ in range(4)]
                qkseq = [(which, wmat, og, cg)
                         for which, wmat, nout in (("q", wq, DC), ("k", wk, RC))
                         for og in range(nout // 4)
                         for cg in range(DC // 4)]

                def qk_load(idx):
                    _, wmat_, og_, cg_ = qkseq[idx]
                    return [ws.load(wmat_[(cg_ * 4 + k) * 128:
                                          (cg_ * 4 + k + 1) * 128,
                                          og_ * 512:(og_ + 1) * 512])
                            for k in range(4)]

                pend_q = qk_load(0)
                for idx, (which, wmat, og, cg) in enumerate(qkseq):
                    wis = pend_q
                    if idx + 1 < len(qkseq):
                        pend_q = qk_load(idx + 1)
                    for k in range(4):
                        cch = cg * 4 + k
                        for j in range(4):
                            emit_group(P, ps_q[j][:, :], ps_q_sl[j],
                                       ws.bufs(wis[k], j * 128,
                                               (j + 1) * 128),
                                       yb.aps(cch),
                                       ws.slots(wis[k]), yb.slot(cch),
                                       first=(cch == 0),
                                       last=(cch == DC - 1))
                    if cg != DC // 4 - 1:
                        continue
                    for j in range(4):
                            oc = og * 4 + j
                            qs_ = oc % 2
                            P.op("scalar",
                                 (lambda o, a: (lambda e: e.copy(out=o, in_=a)))(
                                     qout[qs_][:, :], ps_q[j][:, :]),
                                 reads=[ps_q_sl[j]], writes=[qout_sl[qs_]])
                            if which == "q":
                                dst = q_ap[oc * 128:(oc + 1) * 128,
                                           t * TT:(t + 1) * TT]
                                dslot = q_slot
                            else:
                                dst = bass.AP(kvs, oc * 128 * NT + t * TT,
                                              [[NT, 128], [1, TT]])
                                dslot = kvs_slot
                            P.store(dst, qout[qs_][:, :],
                                    reads=[qout_sl[qs_]], writes=[dslot])
                P.flush(f"qkv_qk{t}")
                if t == 1:
                    ag(0, RN, 0)

            with contextlib.ExitStack() as vctx:
                ps_v = [vctx.enter_context(PS(nc, f"e_pv{t}_{i}", [128, TT], F32))
                        for i in range(8)]
                ps_v_sl = [Slot() for _ in range(8)]
                vo = [vctx.enter_context(SB(nc, f"e_vo{t}_{i}", [128, R], F32))
                      for i in range(2)]
                vo_sl = [P.rslot(vctx, kind="st") for _ in range(2)]
                def v_load(cg):
                    pairs = []
                    for j in range(2):
                        cch = cg * 2 + j
                        wi = ws.load(wv[cch * 128:(cch + 1) * 128, 0:512])
                        wi2 = ws.load(wv[cch * 128:(cch + 1) * 128, 512:1024])
                        pairs.append((cch, wi, wi2))
                    return pairs

                pend_v = v_load(0)
                for cg in range(DC // 2):
                    pairs = pend_v
                    if cg + 1 < DC // 2:
                        pend_v = v_load(cg + 1)
                    for cch, wi, wi2 in pairs:
                        for ts in range(4):
                            for dvh, wii in ((0, wi), (1, wi2)):
                                emit_group(P, ps_v[ts * 2 + dvh][:, :],
                                           ps_v_sl[ts * 2 + dvh],
                                           yb.aps(cch, ts * 128, (ts + 1) * 128),
                                           ws.bufs(wii, 0, 512),
                                           yb.slot(cch), ws.slots(wii),
                                           first=(cch == 0),
                                           last=(cch == DC - 1))
                for ts in range(4):
                    vi = ts % 2
                    for dvh in range(2):
                        P.op("vector",
                             (lambda o, a, b: (lambda e: e.tensor_tensor(
                                 out=o, in0=a, in1=b, op=OP.add)))(
                                 vo[vi][:, dvh * 512:(dvh + 1) * 512],
                                 ps_v[ts * 2 + dvh][:, :],
                                 g["bvb"][:, dvh * 512:(dvh + 1) * 512]),
                             reads=[ps_v_sl[ts * 2 + dvh], gsl["bvb"]],
                             writes=[vo_sl[vi]])
                    P.store(bass.AP(kvs, RN + (t * TT + ts * 128) * R,
                                    [[R, 128], [1, R]]),
                            vo[vi][:, :], reads=[vo_sl[vi]], writes=[kvs_slot])
                P.flush(f"qkv_v{t}")
                if t == 0:
                    ag(RN, TT * R, 8 * RN)
                else:
                    ag(RN + TT * R, TT * R, 12 * RN)



def emit_attention(P, nc, kvf, kvf_slot, q_ap, q_slot, wo, x1, x1_slot,
                   x2, x2_slot, g, gsl):
    split = PREC == "bf16x3"
    with contextlib.ExitStack() as gctx:
        def sb(name, shape, dt=F32):
            return gctx.enter_context(SB(nc, name, shape, dt))

        assert split
        if split:
            ctxb = (sb("g_ctxh", [128, DC * TT], BF16),
                    sb("g_ctxl", [128, DC * TT], BF16))
        else:
            ctxb = (sb("g_ctxf", [128, DC * TT]),)
        ctx_sl = [Slot() for _ in range(DC)]

        ps_st = [gctx.enter_context(PS(nc, f"g_ps{i}", [1, TT], F32))
                 for i in range(4)]
        ps_st_sl = [Slot() for _ in range(4)]

        def ctxslc(c):
            c0, c1 = c * TT, (c + 1) * TT
            return tuple(x[:, c0:c1] for x in ctxb)

        for b in range(2):
            with contextlib.ExitStack() as actx:
                ps_s = [actx.enter_context(PS(nc, f"g_pss{b}_{i}",
                                              [128, 2 * QS], F32))
                        for i in range(2)]
                ps_s_sl = [Slot() for _ in range(2)]
                ps_c = actx.enter_context(PS(nc, f"g_psc{b}", [128, QS], F32))
                ps_c_sl = Slot()
                ps_bc = actx.enter_context(PS(nc, f"g_psbc{b}", [128, QS], F32))
                ps_bc_sl = Slot()
                ps_sro = ps_bc[0:1, :]
                ps_sro_sl = ps_bc_sl

                def asb(name, shape, dt=F32):
                    return actx.enter_context(SB(nc, name, shape, dt))

                kraw = asb(f"g_kraw{b}", [128, S_KEYS // 2])
                kraw_sl = P.rslot(actx, waw=False)
                kh = [asb(f"g_kh{b}_{i}", [128, S_KEYS], BF16) for i in range(2)]
                kl = [asb(f"g_kl{b}_{i}", [128, S_KEYS], BF16) for i in range(2)]
                kh_sl = [Slot() for _ in range(2)]
                kl_sl = [Slot() for _ in range(2)]
                vbuf = [asb(f"g_vbuf{b}_{i}", [128, S_KEYS]) for i in range(2)]
                vbuf_sl = [P.rslot(actx, waw=False) for _ in range(2)]
                qraw = asb(f"g_qraw{b}", [128, TT])
                qraw_sl = P.rslot(actx, waw=False)
                qh = [asb(f"g_qh{b}_{i}", [128, TT], BF16) for i in range(2)]
                ql = [asb(f"g_ql{b}_{i}", [128, TT], BF16) for i in range(2)]
                qh_sl = [Slot() for _ in range(2)]
                ql_sl = [Slot() for _ in range(2)]
                ssb = asb(f"g_ssb{b}", [128, KC * QS])
                ssb_sl = [Slot() for _ in range(KC)]
                sumb = asb(f"g_sumb{b}", [128, QS])
                sumb_sl = Slot()
                invb = asb(f"g_invb{b}", [128, QS])
                invb_sl = Slot()
                cbuf = asb(f"g_cbuf{b}", [128, QS])
                cbuf_sl = Slot()

                def load_kv(h):
                    hb = h % 2
                    HS = S_KEYS // 2
                    for half in range(2):
                        for si in range(4):
                            s = 4 * half + si
                            src = bass.AP(kvf,
                                          s * RN + h * 128 * NT + b * TT,
                                          [[NT, 128], [1, TT]])
                            P.load(kraw[:, si * TT:(si + 1) * TT], src,
                                   reads=[kvf_slot], writes=[kraw_sl])
                        P.op("scalar",
                             (lambda hb_, hf: (lambda e: e.copy(
                                 out=kh[hb_][:, hf * HS:(hf + 1) * HS],
                                 in_=kraw[:, :])))(hb, half),
                             reads=[kraw_sl], writes=[kh_sl[hb]])
                        P.op("vector",
                             (lambda hb_, hf: (lambda e: e.tensor_tensor(
                                 out=kl[hb_][:, hf * HS:(hf + 1) * HS],
                                 in0=kraw[:, :],
                                 in1=kh[hb_][:, hf * HS:(hf + 1) * HS],
                                 op=OP.subtract)))(hb, half),
                             reads=[kraw_sl, kh_sl[hb]], writes=[kl_sl[hb]])
                    for j in range(KC):
                        s, jl = j // 4, j % 4
                        src = bass.AP(kvf,
                                      (8 + 4 * b) * RN + s * (TT * R)
                                      + (jl * 128) * R + h * 128,
                                      [[R, 128], [1, 128]])
                        P.load(vbuf[hb][:, j * 128:(j + 1) * 128], src,
                               reads=[kvf_slot], writes=[vbuf_sl[hb]])

                def load_q(qhn, gb):
                    P.load(qraw[:, :],
                           q_ap[qhn * 128:(qhn + 1) * 128, b * TT:(b + 1) * TT],
                           reads=[q_slot], writes=[qraw_sl])
                    P.op("scalar",
                         (lambda gb_: (lambda e: e.copy(
                             out=qh[gb_][:, :], in_=qraw[:, :])))(gb),
                         reads=[qraw_sl], writes=[qh_sl[gb]])
                    P.op("vector",
                         (lambda gb_: (lambda e: e.tensor_tensor(
                             out=ql[gb_][:, :], in0=qraw[:, :],
                             in1=qh[gb_][:, :], op=OP.subtract)))(gb),
                         reads=[qraw_sl, qh_sl[gb]], writes=[ql_sl[gb]])

                load_kv(0)
                for h in range(NKV):
                    hb = h % 2
                    load_q(4 * h, 0)
                    if h + 1 < NKV:
                        load_kv(h + 1)
                    k_aps = (kh[hb], kl[hb])
                    k_sls = [kh_sl[hb], kl_sl[hb]]
                    for gq in range(GQ):
                        qhn = 4 * h + gq
                        gb = gq % 2
                        if gq + 1 < GQ:
                            load_q(qhn + 1, (gq + 1) % 2)
                        q_aps = (qh[gb], ql[gb])
                        q_sls = [qh_sl[gb], ql_sl[gb]]
                        if ATTN_LVL < 1:
                            continue
                        for qs in range(2):
                            qc = slice(qs * QS, (qs + 1) * QS)
                            for kp in range(KC // 2):
                                pbk = kp % 2
                                for half in range(2):
                                    kc = 2 * kp + half
                                    lhs = tuple(a[:, kc * 128:(kc + 1) * 128]
                                                for a in k_aps)
                                    rhs = tuple(a[:, qc] for a in q_aps)
                                    emit_group(P,
                                               ps_s[pbk][:, half * QS:
                                                         (half + 1) * QS],
                                               ps_s_sl[pbk],
                                               lhs, rhs, k_sls, q_sls,
                                               first=True, last=True)
                                if ATTN_LVL < 2:
                                    continue
                                P.op("scalar",
                                     (lambda o, a: (lambda e: e.activation(
                                         out=o, in_=a, func=AF.Exp,
                                         bias=0.0, scale=ATT_SCALE)))(
                                         ssb[:, kp * 2 * QS:(kp + 1) * 2 * QS],
                                         ps_s[pbk][:, :]),
                                     reads=[ps_s_sl[pbk]], writes=[ssb_sl[kp]])
                                if ATTN_LVL < 3:
                                    continue
                                if kp == 0:
                                    P.op("vector",
                                         lambda e: e.tensor_tensor(
                                             out=sumb[:, :], in0=ssb[:, 0:QS],
                                             in1=ssb[:, QS:2 * QS], op=OP.add),
                                         reads=[ssb_sl[0]], writes=[sumb_sl])
                                else:
                                    for half in range(2):
                                        kc = 2 * kp + half
                                        P.op("vector",
                                             (lambda a: (lambda e: e.tensor_tensor(
                                                 out=sumb[:, :], in0=sumb[:, :],
                                                 in1=a, op=OP.add)))(
                                                 ssb[:, kc * QS:(kc + 1) * QS]),
                                             reads=[ssb_sl[kp], sumb_sl],
                                             writes=[sumb_sl])
                            if ATTN_LVL < 5:
                                continue
                            for kc in range(KC - 1, -1, -1):
                                emit_group(P, ps_c[:, :], ps_c_sl,
                                           (vbuf[hb][:, kc * 128:(kc + 1) * 128],),
                                           (ssb[:, kc * QS:(kc + 1) * QS],),
                                           [vbuf_sl[hb]], [ssb_sl[kc // 2]],
                                           first=(kc == KC - 1), last=(kc == 0))
                            P.op("tensor",
                                 lambda e: e.matmul(ps_sro, g["ones_kcol"][:, :],
                                                    sumb[:, :], start=True, stop=True),
                                 reads=[sumb_sl, gsl["ones_kcol"]], writes=[ps_sro_sl])
                            P.op("vector",
                                 lambda e: e.reciprocal(invb[0:1, :], ps_sro),
                                 reads=[ps_sro_sl], writes=[invb_sl])
                            P.op("tensor",
                                 lambda e: e.matmul(ps_bc[:, :], g["ones_brow"][:, :],
                                                    invb[0:1, :], start=True, stop=True),
                                 reads=[invb_sl, gsl["ones_brow"]], writes=[ps_bc_sl])
                            P.op("scalar",
                                 lambda e: e.copy(out=invb[:, :], in_=ps_bc[:, :]),
                                 reads=[ps_bc_sl], writes=[invb_sl])
                            P.op("vector",
                                 lambda e: e.tensor_tensor(
                                     out=cbuf[:, :], in0=ps_c[:, :],
                                     in1=invb[:, :], op=OP.mult),
                                 reads=[ps_c_sl, invb_sl], writes=[cbuf_sl])
                            dst0 = qhn * TT + qs * QS
                            P.op("scalar",
                                 (lambda o: (lambda e: e.copy(out=o, in_=cbuf[:, :])))(
                                     ctxb[0][:, dst0:dst0 + QS]),
                                 reads=[cbuf_sl], writes=[ctx_sl[qhn]])
                            if split:
                                P.op("vector",
                                     (lambda o, hh: (lambda e: e.tensor_tensor(
                                         out=o, in0=cbuf[:, :], in1=hh,
                                         op=OP.subtract)))(
                                         ctxb[1][:, dst0:dst0 + QS],
                                         ctxb[0][:, dst0:dst0 + QS]),
                                     reads=[cbuf_sl], writes=[ctx_sl[qhn]])
                P.flush(f"attn{b}")

            if ATTN_LVL < 6:
                continue
            with contextlib.ExitStack() as wctx:
                wos = WStream(P, wctx, "g_wo", 10, TT, split=True, group=2)
                x1c = [wctx.enter_context(SB(nc, f"g_x1c{b}_{i}", [128, TT], F32))
                       for i in range(2)]
                x2o = [wctx.enter_context(SB(nc, f"g_x2o{b}_{i}", [128, TT], F32))
                       for i in range(2)]
                x2q = [wctx.enter_context(SB(nc, f"g_x2q{b}_{i}", [128, TT], F32))
                       for i in range(2)]
                x1c_sl = [P.rslot(wctx, waw=False) for _ in range(2)]
                x2o_sl = [P.rslot(wctx, kind="st") for _ in range(2)]
                x2q_sl = [Slot() for _ in range(2)]
                ps_w = [wctx.enter_context(PS(nc, f"g_pw{b}_{i}", [128, TT], F32))
                        for i in range(4)]
                ps_w_sl = [Slot() for _ in range(4)]
                woseq = [(dog, cg) for dog in range(8)
                         for cg in range(DC // 2)]

                def wo_load(idx):
                    dog_, cg_ = woseq[idx]
                    return [wos.load(wo[(cg_ * 2 + k) * 128:
                                        (cg_ * 2 + k + 1) * 128,
                                        dog_ * 512:(dog_ + 1) * 512])
                            for k in range(2)]

                pend_w = wo_load(0)
                for idx, (dog, cg) in enumerate(woseq):
                    wis = pend_w
                    if idx + 1 < len(woseq):
                        pend_w = wo_load(idx + 1)
                    for k in range(2):
                        cch = cg * 2 + k
                        for do in range(4):
                            emit_group(P, ps_w[do][:, :], ps_w_sl[do],
                                       wos.bufs(wis[k], do * 128,
                                                (do + 1) * 128),
                                       ctxslc(cch),
                                       wos.slots(wis[k]), [ctx_sl[cch]],
                                       first=(cch == 0),
                                       last=(cch == DC - 1))
                    if cg != DC // 2 - 1:
                        continue
                    for do in range(4):
                        dd = dog * 4 + do
                        xs = dd % 2
                        P.load(x1c[xs][:, :],
                               x1[dd * 128:(dd + 1) * 128, b * TT:(b + 1) * TT],
                               reads=[x1_slot], writes=[x1c_sl[xs]])
                        P.op("vector",
                             (lambda o, a, bb: (lambda e: e.tensor_tensor(
                                 out=o, in0=a, in1=bb, op=OP.add)))(
                                 x2o[xs][:, :], ps_w[do][:, :], x1c[xs][:, :]),
                             reads=[ps_w_sl[do], x1c_sl[xs]], writes=[x2o_sl[xs]])
                        P.store(x2[dd * 128:(dd + 1) * 128, b * TT:(b + 1) * TT],
                                x2o[xs][:, :], reads=[x2o_sl[xs]], writes=[x2_slot])
                        P.op("vector",
                             (lambda o, a: (lambda e: e.tensor_tensor(
                                 out=o, in0=a, in1=a, op=OP.mult)))(
                                 x2q[xs][:, :], x2o[xs][:, :]),
                             reads=[x2o_sl[xs]], writes=[x2q_sl[xs]])
                        first = (dog == 0 and do == 0)
                        last = (dog == 7 and do == 3)
                        emit_group(P, ps_st[2 * b][:, :], ps_st_sl[2 * b],
                                   (g["ones_kcol"][:, :],), (x2o[xs][:, :],),
                                   [gsl["ones_kcol"]], [x2o_sl[xs]],
                                   first=first, last=last)
                        emit_group(P, ps_st[2 * b + 1][:, :], ps_st_sl[2 * b + 1],
                                   (g["ones_kcol"][:, :],), (x2q[xs][:, :],),
                                   [gsl["ones_kcol"]], [x2q_sl[xs]],
                                   first=first, last=last)
                P.flush(f"wo{b}")
        with PS(nc, "g_bc", [128, NT], F32) as ps_b:
            finish_stats(P, [ps_st[0], ps_st[2]], [ps_st_sl[0], ps_st_sl[2]],
                         [ps_st[1], ps_st[3]], [ps_st_sl[1], ps_st_sl[3]],
                         ps_b, g, gsl)
            P.flush("attn_st")



def _prep_inputs(inputs):
    x = np.asarray(inputs["L0_attn_hidden_states"], dtype=np.float32)
    wq = np.ascontiguousarray(np.asarray(inputs["L0_attn_Wq"], np.float32).T)
    wk = np.ascontiguousarray(np.asarray(inputs["L0_attn_Wk"], np.float32).T)
    wv = np.ascontiguousarray(np.asarray(inputs["L0_attn_Wv"], np.float32).T)
    wo = np.ascontiguousarray(np.asarray(inputs["L0_attn_Wo"], np.float32).T)
    bv = np.asarray(inputs["L0_attn_bv"], np.float32)
    u1 = np.ascontiguousarray(np.asarray(inputs["L0_ffn_U1"], np.float32))
    v1 = np.ascontiguousarray(np.asarray(inputs["L0_ffn_V1"], np.float32))
    u2 = np.ascontiguousarray(np.asarray(inputs["L0_ffn_U2"], np.float32))
    v2 = np.ascontiguousarray(np.asarray(inputs["L0_ffn_V2"], np.float32))
    b1 = np.asarray(inputs["L0_ffn_b1"], np.float32)
    b2 = np.asarray(inputs["L0_ffn_b2"], np.float32)

    b1r = np.ascontiguousarray(b1.reshape(2 * IC, 128).T)
    b2r = np.ascontiguousarray(b2.reshape(DC, 128).T)
    bvb = np.ascontiguousarray(np.broadcast_to(bv[None, :], (128, R)))

    shared = dict(u1=u1, v1=v1, u2=u2, v2=v2, b1r=b1r, b2r=b2r, bvb=bvb,
                  wq=wq, wk=wk, wv=wv, wo=wo)
    in_maps = []
    for c in range(N_CORES):
        xc = np.concatenate(
            [x[0, c * TT:(c + 1) * TT, :], x[1, c * TT:(c + 1) * TT, :]], axis=0)
        xc = np.ascontiguousarray(xc.T)
        m = dict(shared)
        m["x"] = xc
        in_maps.append(m)
    return in_maps


def kernel(**inputs):
    in_maps = _prep_inputs(inputs)
    nc = build_nc()
    res = run_bass_kernel_spmd(nc, in_maps, core_ids=list(range(N_CORES)))
    out = np.empty((2, N_CORES * TT, D), dtype=np.float32)
    for c in range(N_CORES):
        oc = res.results[c]["out"]
        out[0, c * TT:(c + 1) * TT, :] = oc[:, 0:TT].T
        out[1, c * TT:(c + 1) * TT, :] = oc[:, TT:NT].T
    return out


# version 14 (final)
# speedup vs baseline: 1.0044x; 1.0044x over previous
import contextlib

import numpy as np

import concourse.bass as bass
import concourse.mybir as mybir
from concourse.alu_op_type import AluOpType
from concourse.bass_isa import ReduceOp
from concourse.bass_utils import run_bass_kernel_spmd

F32 = mybir.dt.float32
BF16 = mybir.dt.bfloat16
AF = mybir.ActivationFunctionType
AX = mybir.AxisListType
OP = AluOpType

D = 4096
R = 1024
II = 11008
NT = 1024
TT = 512
QS = 256
DH = 128
NKV = 8
NQ = 32
GQ = 4
DC = D // 128
RC = R // 128
IC = II // 128
EPS = 1e-5
ATT_SCALE = DH ** -0.5

PREC = "bf16x3"
STOP_AFTER = None
DBG = False
AG_FAKE = False
ATTN_LVL = 6
N_CORES = 8
RN = R * NT
S_KEYS = N_CORES * TT
KC = S_KEYS // 128




_NAME_N = [0]


def _un(base):
    _NAME_N[0] += 1
    return f"{base}__{_NAME_N[0]}"


def SB(nc, name, shape, dt=F32):
    return nc.sbuf_tensor(_un(name), shape, dt)


def PS(nc, name, shape, dt=F32):
    return nc.psum_tensor(_un(name), shape, dt)


class Sem:
    def __init__(self, ctx, nc, name):
        self.name = name
        self.h = ctx.enter_context(nc.semaphore(name))
        self.v = 0
        self.kind = None


class Slot:
    __slots__ = ("w", "readers", "const", "waw", "sem")

    def __init__(self, const=False, waw=True):
        self.w = {}
        self.readers = []
        self.const = const
        self.waw = waw
        self.sem = None

    def pre_write(self):
        ws = list(self.readers)
        if self.waw:
            ws.extend(self.w.items())
        return ws

    def did_write(self, sem, val):
        if sem is not None:
            self.w[sem] = max(self.w.get(sem, 0), val)
        self.readers = []

    def pre_read(self):
        return list(self.w.items())

    def did_read(self, sem, val):
        if not self.const and sem is not None:
            self.readers.append((sem, val))


N_POOL = 64


class Prog:
    ENGS = ("sync", "vector", "scalar", "tensor", "gpsimd")
    OWN = {"vector": ("lo", 1), "scalar": ("hi", 1), "tensor": ("mm", 1),
           "gpsimd": ("gp", 1)}

    def __init__(self, nc, ctx):
        self.nc = nc
        self.sem = {n: Sem(ctx, nc, n) for n in ("hi", "lo", "mm", "gp", "cc")}
        for i in range(N_POOL):
            s = Sem(ctx, nc, f"q{i}")
            self.sem[s.name] = s
        pool = [self.sem[f"q{i}"] for i in range(N_POOL)]
        self.free = {"ld": pool[:44], "st": pool[44:]}
        self.pending = []
        self.dirty = set()
        self.ops = {e: [] for e in self.ENGS}
        self.seen = {e: {} for e in self.ENGS}
        self.nblocks = 0

    def acquire(self, kind):
        if not self.free[kind]:
            raise RuntimeError(f"sem pool exhausted ({kind})")
        s = self.free[kind].pop()
        s.kind = kind
        return s

    def release(self, *slots):
        for sl in slots:
            if sl.sem is not None:
                self.pending.append(sl.sem)
                sl.sem = None

    def rslot(self, stack, kind="ld", **kw):
        sl = Slot(**kw)
        sl.sem = self.acquire(kind)
        stack.callback(self.release, sl)
        return sl

    def v(self, name):
        return self.sem[name].v

    def _schedule(self, eng, fn, sem, inc, reads, writes, extra):
        waits = {}
        for s in reads:
            for nm, vv in s.pre_read():
                waits[nm] = max(waits.get(nm, 0), vv)
        for s in writes:
            for nm, vv in s.pre_write():
                waits[nm] = max(waits.get(nm, 0), vv)
        for nm, vv in extra:
            waits[nm] = max(waits.get(nm, 0), vv)
        seen = self.seen[eng]
        own_nm = self.OWN[eng][0] if eng in self.OWN else None
        emit_waits = []
        for nm, vv in waits.items():
            if vv <= 0 or seen.get(nm, 0) >= vv:
                continue
            if nm == own_nm and vv <= self.sem[nm].v:
                continue
            emit_waits.append((self.sem[nm].h, vv))
            seen[nm] = vv
        if sem is not None:
            sem.v += inc
            val = sem.v
            nm_out = sem.name
        else:
            val, nm_out = None, None
        self.ops[eng].append((fn, tuple(emit_waits),
                              (sem.h, inc) if sem is not None else None))
        for s in reads:
            s.did_read(nm_out, val)
        for s in writes:
            s.did_write(nm_out, val)
        return val

    def op(self, eng, fn, reads=(), writes=(), extra=()):
        nm, inc = self.OWN[eng]
        return self._schedule(eng, fn, self.sem[nm], inc, reads, writes, extra)

    def opn(self, eng, fn, reads=(), writes=(), extra=()):
        return self._schedule(eng, fn, None, 0, reads, writes, extra)

    def load(self, out_ap, in_ap, reads=(), writes=(), extra=()):
        dst = writes[0]
        if dst.sem is None:
            dst.sem = self.acquire("ld")
        self.dirty.add(dst.sem.name)
        fn = (lambda o, i: (lambda e: e.dma_start(out=o, in_=i)))(out_ap, in_ap)
        return self._schedule("sync", fn, dst.sem, 16, reads, writes, extra)

    def store(self, out_ap, in_ap, reads=(), writes=(), extra=()):
        src = reads[0]
        if src.sem is None:
            src.sem = self.acquire("st")
        self.dirty.add(src.sem.name)
        fn = (lambda o, i: (lambda e: e.dma_start(out=o, in_=i)))(out_ap, in_ap)
        return self._schedule("gpsimd", fn, src.sem, 16, reads, writes, extra)

    def collective(self, fn, reads=(), writes=()):
        return self._schedule("gpsimd", fn, self.sem["cc"], 1, reads, writes, ())

    def flush(self, name=None):
        name = _un(name or "b") + f"_{self.nblocks}"
        self.nblocks += 1
        drain = [("%s" % nm, self.sem[nm].v) for nm in sorted(self.dirty)]
        if drain:
            self._schedule("gpsimd", lambda e: e.nop(), None, 0, (), (), drain)
        with self.nc.Block(name) as blk:
            for eng in self.ENGS:
                ops = self.ops[eng]
                if not ops:
                    continue

                def body(e, ops=ops):
                    for fn, waits, inc in ops:
                        for h, vv in waits:
                            e.wait_ge(h, vv)
                        instr = fn(e)
                        if inc is not None:
                            instr.then_inc(inc[0], inc[1])

                getattr(blk, eng)(body)
        self.ops = {e: [] for e in self.ENGS}
        for e in self.ENGS:
            for nm in ("hi", "lo", "mm", "gp"):
                self.seen[e][nm] = self.sem[nm].v
            for nm, vv in drain:
                self.seen[e][nm] = vv
        self.dirty.clear()
        for s in self.pending:
            self.free[s.kind].append(s)
        self.pending = []


def npass():
    return 1 if PREC == "f32" else 3


def emit_group(P, psum_ap, psum_slot, lhs, rhs, lhs_slots, rhs_slots,
               first, last):
    if len(lhs) == 1 and len(rhs) == 1:
        combos = [(lhs[0], rhs[0], first, last)]
    else:
        combos = [(lhs[0], rhs[0], first, False),
                  (lhs[1], rhs[0], False, False),
                  (lhs[0], rhs[1], False, last)]
    v = None
    n = len(combos)
    for i, (a, b, st_, sp_) in enumerate(combos):
        fn = (lambda pa, aa, bb, s1, s2: (lambda e: e.matmul(
            pa, aa, bb, start=s1, stop=s2)))(psum_ap, a, b, st_, sp_)
        if i == 0 and n > 1:
            P.opn("tensor", fn,
                  reads=list(lhs_slots) + list(rhs_slots),
                  writes=[psum_slot] if first else ())
        elif i < n - 1:
            P.opn("tensor", fn)
        else:
            v = P.op("tensor", fn,
                     reads=(list(lhs_slots) + list(rhs_slots)) if n == 1 else (),
                     writes=([psum_slot] if first else ()) if n == 1 else ())
    psum_slot.did_write("mm", v)
    for s in list(lhs_slots) + list(rhs_slots):
        s.did_read("mm", v)
    return v


class WStream:

    def __init__(self, P, ctx, name, slots, width, split, dtype=F32, group=1):
        assert slots % group == 0
        self.P = P
        self.nslots = slots
        self.width = width
        self.group = group
        self.split = split and PREC == "bf16x3"
        ngrp = slots // group
        nc = P.nc
        self.raw = ctx.enter_context(SB(nc, f"{name}_r", [128, slots * width],
                                        dtype))
        self.raw_sl = [P.rslot(ctx, waw=False) for _ in range(ngrp)]
        if self.split:
            self.hib = ctx.enter_context(SB(nc, f"{name}_h",
                                            [128, slots * width], BF16))
            self.lob = ctx.enter_context(SB(nc, f"{name}_l",
                                            [128, slots * width], BF16))
            self.hi_sl = [Slot() for _ in range(ngrp)]
            self.lo_sl = [Slot() for _ in range(ngrp)]
        self.n = 0

    def load(self, in_ap, src_slot=None):
        P, i = self.P, self.n
        s = i % self.nslots
        gg = s // self.group
        srcs = [src_slot] if src_slot is not None else []
        c0 = s * self.width
        P.load(self.raw[:, c0:c0 + self.width],
               in_ap, reads=srcs, writes=[self.raw_sl[gg]])
        if (s % self.group) == self.group - 1 and self.split:
            a = gg * self.group * self.width
            b = (gg + 1) * self.group * self.width
            P.op("scalar",
                 (lambda o, aa: (lambda e: e.copy(out=o, in_=aa)))(
                     self.hib[:, a:b], self.raw[:, a:b]),
                 reads=[self.raw_sl[gg]], writes=[self.hi_sl[gg]])
            P.op("vector",
                 (lambda o, aa, bb: (lambda e: e.tensor_tensor(
                     out=o, in0=aa, in1=bb, op=OP.subtract)))(
                     self.lob[:, a:b], self.raw[:, a:b], self.hib[:, a:b]),
                 reads=[self.raw_sl[gg], self.hi_sl[gg]],
                 writes=[self.lo_sl[gg]])
        self.n += 1
        return i

    def bufs(self, i, c0, c1):
        s = i % self.nslots
        base = s * self.width
        if self.split:
            return (self.hib[:, base + c0:base + c1],
                    self.lob[:, base + c0:base + c1])
        return (self.raw[:, base + c0:base + c1],)

    def slots(self, i):
        gg = (i % self.nslots) // self.group
        if self.split:
            return (self.hi_sl[gg], self.lo_sl[gg])
        return (self.raw_sl[gg],)



def build_nc():
    nc = bass.Bass("TRN2", target_bir_lowering=False)
    ctx = contextlib.ExitStack()

    def din(name, shape):
        return nc.dram_tensor(name, shape, F32, kind="ExternalInput")

    x_in = din("x", [D, NT])
    u1 = din("u1", [D, R])
    v1 = din("v1", [R, 2 * II])
    u2 = din("u2", [II, R])
    v2 = din("v2", [R, D])
    b1r = din("b1r", [128, 2 * IC])
    b2r = din("b2r", [128, DC])
    bvb_d = din("bvb", [128, R])
    wq = din("wq", [D, D])
    wk = din("wk", [D, R])
    wv = din("wv", [D, R])
    wo = din("wo", [D, D])
    out_t = nc.dram_tensor("out", [D, NT], F32, kind="ExternalOutput")

    x1_t = nc.dram_tensor("x1_d", [D, NT], F32)
    x2_t = nc.dram_tensor("x2_d", [D, NT], F32)
    q_t = nc.dram_tensor("q_d", [D, NT], F32)
    if PREC == "f32":
        act_t = nc.dram_tensor("act_d", [IC, 128, NT], F32)
    else:
        act_t = nc.dram_tensor("act_d", [2, IC, 128, NT], BF16)
    kvs_t = nc.dram_tensor("kvs", [2 * RN], F32)
    kvf_t = nc.dram_tensor("kvf", [N_CORES * 2 * RN], F32)

    dsl = {n: Slot(waw=False) for n in ("x1", "x2", "q", "act", "kvs",
                                        "kvf_k", "kvf_v0", "kvf_v1")}

    with ctx:
        P = Prog(nc, ctx)

        def sb(name, shape, dt=F32):
            return ctx.enter_context(SB(nc, name, shape, dt))

        g = {}
        g["ones_kcol"] = sb("ones_kcol", [128, 1])
        g["ones_brow"] = sb("ones_brow", [1, 128])
        g["b1s"] = sb("b1s", [128, 2 * IC])
        g["b2h"] = sb("b2h", [128, DC])
        g["bvb"] = sb("bvb_s", [128, R])
        g["mb"] = sb("mb", [128, NT])
        g["rstdb"] = sb("rstdb", [128, NT])
        g["srow"] = sb("srow", [1, NT])
        g["srow2"] = sb("srow2", [1, NT])
        g["srow3"] = sb("srow3", [1, NT])
        gsl = {n: Slot() for n in g}
        gsl["ones_kcol"].const = True
        gsl["ones_brow"].const = True
        gsl["b1s"].const = True
        gsl["b2h"].const = True
        gsl["bvb"].const = True

        P.op("vector", lambda e: e.memset(g["ones_kcol"][:, :], 1.0),
             writes=[gsl["ones_kcol"]])
        P.op("vector", lambda e: e.memset(g["ones_brow"][:, :], 1.0),
             writes=[gsl["ones_brow"]])
        P.load(g["b1s"][:, :], b1r.ap()[:, :], writes=[gsl["b1s"]])
        P.load(g["b2h"][:, :], b2r.ap()[:, :], writes=[gsl["b2h"]])
        P.load(g["bvb"][:, :], bvb_d.ap()[:, :], writes=[gsl["bvb"]])
        P.op("vector", lambda e: e.tensor_scalar_mul(
            g["b2h"][:, :], g["b2h"][:, :], 0.5),
            reads=[gsl["b2h"]], writes=[gsl["b2h"]])
        gsl["b2h"].const = True

        emit_stats_pass(P, ctx, nc, x_in.ap(), None, g, gsl)
        P.flush("p0")

        if STOP_AFTER != "p0":
            emit_ffn(P, nc, x_src=x_in.ap(), x_src_slot=Slot(const=True),
                     x_dst=x1_t.ap(), x_dst_slot=dsl["x1"],
                     u1=u1.ap(), v1=v1.ap(), u2=u2.ap(), v2=v2.ap(),
                     act_t=act_t, act_slot=dsl["act"],
                     g=g, gsl=gsl, next_stats=True)

        if STOP_AFTER not in ("p0", "ffn1"):
            emit_qkv(P, nc, x1=x1_t.ap(), x1_slot=dsl["x1"],
                     wq=wq.ap(), wk=wk.ap(), wv=wv.ap(),
                     q_ap=q_t.ap(), q_slot=dsl["q"],
                     kvs=kvs_t, kvs_slot=dsl["kvs"],
                     kvf=kvf_t, kvf_slots=(dsl["kvf_k"], dsl["kvf_v0"],
                                           dsl["kvf_v1"]), g=g, gsl=gsl)

        if STOP_AFTER not in ("p0", "ffn1", "qkv", "ag"):
            emit_attention(P, nc, kvf=kvf_t,
                           kvf_slots=(dsl["kvf_k"], dsl["kvf_v0"],
                                      dsl["kvf_v1"]),
                           q_ap=q_t.ap(), q_slot=dsl["q"],
                           wo=wo.ap(), x1=x1_t.ap(), x1_slot=dsl["x1"],
                           x2=x2_t.ap(), x2_slot=dsl["x2"], g=g, gsl=gsl)

        out_sl = Slot(waw=False)
        if STOP_AFTER is None:
            emit_ffn(P, nc, x_src=x2_t.ap(), x_src_slot=dsl["x2"],
                     x_dst=out_t.ap(), x_dst_slot=out_sl,
                     u1=u1.ap(), v1=v1.ap(), u2=u2.ap(), v2=v2.ap(),
                     act_t=act_t, act_slot=dsl["act"],
                     g=g, gsl=gsl, next_stats=False)

        if DBG:
            ranks = {"p0": 0, "ffn1": 1, "qkv": 2, "ag": 3, "attn": 4, None: 5}
            done = ranks[STOP_AFTER]
            dbg_specs = []
            if done >= 1:
                dbg_specs += [("dbg_x1", x1_t, dsl["x1"])]
                if PREC == "f32":
                    dbg_specs += [("dbg_act", act_t, dsl["act"])]
            if done >= 2:
                dbg_specs += [("dbg_q", q_t, dsl["q"]),
                              ("dbg_kvs", kvs_t, dsl["kvs"])]
            if done >= 3:
                dbg_specs += [("dbg_kvf", kvf_t, dsl["kvf"])]
            if done >= 4 and ATTN_LVL >= 6:
                dbg_specs += [("dbg_x2", x2_t, dsl["x2"])]
            for nm, srct, ssl in dbg_specs:
                dt_ = nc.dram_tensor(nm, list(srct.shape), F32,
                                     kind="ExternalOutput")
                P.store(dt_.ap(), srct.ap(), reads=[Slot(), ssl])
            with SB(nc, "dbg_sb", [128, 2 * NT]) as dsb:
                dbs = nc.dram_tensor("dbg_stats", [128, 2 * NT], F32,
                                     kind="ExternalOutput")
                sl = Slot()
                P.op("vector", lambda e: e.tensor_copy(
                    out=dsb[:, 0:NT], in_=g["mb"][:, :]),
                    reads=[gsl["mb"]], writes=[sl])
                P.op("vector", lambda e: e.tensor_copy(
                    out=dsb[:, NT:2 * NT], in_=g["rstdb"][:, :]),
                    reads=[gsl["rstdb"]], writes=[sl])
                P.store(dbs.ap(), dsb[:, :], reads=[sl])

        P.op("gpsimd", lambda e: e.nop(), extra=list(out_sl.w.items()))
        P.flush("fin")

    return nc



def emit_stats_pass(P, ctx, nc, x_ap, x_slot, g, gsl):
    with contextlib.ExitStack() as c:
        xc = [c.enter_context(SB(nc, f"st_xc{i}", [128, TT], F32))
              for i in range(4)]
        xsq = [c.enter_context(SB(nc, f"st_xq{i}", [128, TT], F32))
               for i in range(2)]
        xc_sl = [P.rslot(c, waw=False) for _ in range(4)]
        xsq_sl = [Slot() for _ in range(2)]
        ps = [c.enter_context(PS(nc, f"st_p{i}", [1, TT], F32))
              for i in range(4)]
        ps_sl = [Slot() for _ in range(4)]
        ps_b = c.enter_context(PS(nc, "st_b", [128, NT], F32))
        for t in range(2):
            for cch in range(DC):
                i = t * DC + cch
                xs, qs_ = i % 4, i % 2
                P.load(xc[xs][:, :],
                       x_ap[cch * 128:(cch + 1) * 128, t * TT:(t + 1) * TT],
                       reads=[x_slot] if x_slot else (), writes=[xc_sl[xs]])
                P.op("vector",
                     (lambda o, a: (lambda e: e.tensor_tensor(
                         out=o, in0=a, in1=a, op=OP.mult)))(
                         xsq[qs_][:, :], xc[xs][:, :]),
                     reads=[xc_sl[xs]], writes=[xsq_sl[qs_]])
                first, last = cch == 0, cch == DC - 1
                mm1 = (lambda pa, oa, ra, f_, l_: (lambda e: e.matmul(
                    pa, oa, ra, start=f_, stop=l_)))(
                    ps[2 * t][:, :], g["ones_kcol"][:, :], xc[xs][:, :],
                    first, last)
                P.op("tensor", mm1, reads=[gsl["ones_kcol"], xc_sl[xs]],
                     writes=[ps_sl[2 * t]] if first else ())
                if not first:
                    ps_sl[2 * t].did_write("mm", P.v("mm"))
                mm2 = (lambda pa, oa, ra, f_, l_: (lambda e: e.matmul(
                    pa, oa, ra, start=f_, stop=l_)))(
                    ps[2 * t + 1][:, :], g["ones_kcol"][:, :], xsq[qs_][:, :],
                    first, last)
                P.op("tensor", mm2, reads=[gsl["ones_kcol"], xsq_sl[qs_]],
                     writes=[ps_sl[2 * t + 1]] if first else ())
                if not first:
                    ps_sl[2 * t + 1].did_write("mm", P.v("mm"))
        finish_stats(P, [ps[0], ps[2]], [ps_sl[0], ps_sl[2]],
                     [ps[1], ps[3]], [ps_sl[1], ps_sl[3]], ps_b, g, gsl)


def finish_stats(P, ps_sums, ps_sums_sl, ps_ssqs, ps_ssqs_sl, ps_b, g, gsl):
    ps_b_sl = Slot()
    nT = len(ps_sums)
    for t in range(nT):
        cols = slice(t * TT, (t + 1) * TT)
        P.op("vector",
             (lambda o, a: (lambda e: e.tensor_scalar(
                 out=o, in0=a, scalar1=1.0 / D, scalar2=None, op0=OP.mult)))(
                 g["srow"][0:1, cols], ps_sums[t][:, :]),
             reads=[ps_sums_sl[t]], writes=[gsl["srow"]])
        P.op("vector",
             (lambda o, a: (lambda e: e.tensor_scalar(
                 out=o, in0=a, scalar1=1.0 / D, scalar2=EPS,
                 op0=OP.mult, op1=OP.add)))(
                 g["srow2"][0:1, cols], ps_ssqs[t][:, :]),
             reads=[ps_ssqs_sl[t]], writes=[gsl["srow2"]])
    nc_ = nT * TT
    P.op("vector", lambda e: e.tensor_tensor(
        out=g["srow3"][0:1, 0:nc_], in0=g["srow"][0:1, 0:nc_],
        in1=g["srow"][0:1, 0:nc_], op=OP.mult),
        reads=[gsl["srow"]], writes=[gsl["srow3"]])
    P.op("vector", lambda e: e.tensor_tensor(
        out=g["srow2"][0:1, 0:nc_], in0=g["srow2"][0:1, 0:nc_],
        in1=g["srow3"][0:1, 0:nc_], op=OP.subtract),
        reads=[gsl["srow2"], gsl["srow3"]], writes=[gsl["srow2"]])
    P.op("scalar", lambda e: e.activation(
        out=g["srow3"][0:1, 0:nc_], in_=g["srow2"][0:1, 0:nc_], func=AF.Sqrt),
        reads=[gsl["srow2"]], writes=[gsl["srow3"]])
    P.op("vector", lambda e: e.reciprocal(
        g["srow2"][0:1, 0:nc_], g["srow3"][0:1, 0:nc_]),
        reads=[gsl["srow3"]], writes=[gsl["srow2"]])
    for dst, dsl_, src, ssl in ((g["mb"], gsl["mb"], g["srow"], gsl["srow"]),
                                (g["rstdb"], gsl["rstdb"], g["srow2"], gsl["srow2"])):
        for t in range(nT):
            cols = slice(t * TT, (t + 1) * TT)
            P.op("tensor",
                 (lambda pa, oa, ra: (lambda e: e.matmul(
                     pa, oa, ra, start=True, stop=True)))(
                     ps_b[:, cols], g["ones_brow"][:, :], src[0:1, cols]),
                 reads=[gsl["ones_brow"], ssl], writes=[ps_b_sl])
            P.op("scalar",
                 (lambda o, a: (lambda e: e.copy(out=o, in_=a)))(
                     dst[:, cols], ps_b[:, cols]),
                 reads=[ps_b_sl], writes=[dsl_])



class YBufs:
    def __init__(self, P, ctx, nc, name):
        self.split = PREC == "bf16x3"
        if self.split:
            self.h = ctx.enter_context(SB(nc, f"{name}_h", [128, DC * TT], BF16))
            self.l = ctx.enter_context(SB(nc, f"{name}_l", [128, DC * TT], BF16))
        else:
            self.f = ctx.enter_context(SB(nc, f"{name}_f", [128, DC * TT], F32))
        self.sl = [Slot() for _ in range(DC)]

    def aps(self, c, a=0, b=TT):
        c0, c1 = c * TT + a, c * TT + b
        if self.split:
            return (self.h[:, c0:c1], self.l[:, c0:c1])
        return (self.f[:, c0:c1],)

    def slot(self, c):
        return (self.sl[c],)


class YScratch:
    def __init__(self, P, ctx, nc, name):
        self.xc = [ctx.enter_context(SB(nc, f"{name}_xc{i}", [128, TT], F32))
                   for i in range(4)]
        self.yt = [ctx.enter_context(SB(nc, f"{name}_yt{i}", [128, TT], F32))
                   for i in range(2)]
        self.xc_sl = [P.rslot(ctx, waw=False) for _ in range(4)]
        self.yt_sl = [Slot() for _ in range(2)]


def emit_build_y(P, nc, ys, x_ap, x_slot, t, yb, g, gsl):
    if True:
        xc, yt, xc_sl, yt_sl = ys.xc, ys.yt, ys.xc_sl, ys.yt_sl
        tcols = slice(t * TT, (t + 1) * TT)
        for cch in range(DC):
            xs, ts_ = cch % 4, cch % 2
            P.load(xc[xs][:, :],
                   x_ap[cch * 128:(cch + 1) * 128, tcols],
                   reads=[x_slot], writes=[xc_sl[xs]])
            P.op("vector",
                 (lambda o, a, b: (lambda e: e.tensor_tensor(
                     out=o, in0=a, in1=b, op=OP.subtract)))(
                     yt[ts_][:, :], xc[xs][:, :], g["mb"][:, tcols]),
                 reads=[xc_sl[xs], gsl["mb"]], writes=[yt_sl[ts_]])
            if not yb.split:
                P.op("vector",
                     (lambda o, a, b: (lambda e: e.tensor_tensor(
                         out=o, in0=a, in1=b, op=OP.mult)))(
                         yb.aps(cch)[0], yt[ts_][:, :], g["rstdb"][:, tcols]),
                     reads=[yt_sl[ts_], gsl["rstdb"]], writes=[yb.sl[cch]])
            else:
                P.op("vector",
                     (lambda o, a, b: (lambda e: e.tensor_tensor(
                         out=o, in0=a, in1=b, op=OP.mult)))(
                         yt[ts_][:, :], yt[ts_][:, :], g["rstdb"][:, tcols]),
                     reads=[yt_sl[ts_], gsl["rstdb"]], writes=[yt_sl[ts_]])
                P.op("scalar",
                     (lambda o, a: (lambda e: e.copy(out=o, in_=a)))(
                         yb.aps(cch)[0], yt[ts_][:, :]),
                     reads=[yt_sl[ts_]], writes=[yb.sl[cch]])
                P.op("vector",
                     (lambda o, a, b: (lambda e: e.tensor_tensor(
                         out=o, in0=a, in1=b, op=OP.subtract)))(
                         yb.aps(cch)[1], yt[ts_][:, :], yb.aps(cch)[0]),
                     reads=[yt_sl[ts_]], writes=[yb.sl[cch]])



def emit_ffn(P, nc, x_src, x_src_slot, x_dst, x_dst_slot, u1, v1, u2, v2,
             act_t, act_slot, g, gsl, next_stats):
    split = PREC == "bf16x3"
    with contextlib.ExitStack() as fctx:
        if split:
            pb_h = fctx.enter_context(SB(nc, "pf_h", [128, RC * NT], BF16))
            pb_l = fctx.enter_context(SB(nc, "pf_l", [128, RC * NT], BF16))
            pb = (pb_h, pb_l)
        else:
            pb = (fctx.enter_context(SB(nc, "pf", [128, RC * NT], F32)),)
        pb_sl = [Slot() for _ in range(RC * 2)]

        def pslc(r, t, a=0, b=TT):
            c0 = r * NT + t * TT + a
            c1 = r * NT + t * TT + b
            return tuple(x[:, c0:c1] for x in pb)

        with contextlib.ExitStack() as actx:
            yb = YBufs(P, actx, nc, "a_y")
            ys = YScratch(P, actx, nc, "a_ys")
            u1s = WStream(P, actx, "a_u1", 8, R, split=True, group=2)
            ps_p = [actx.enter_context(PS(nc, f"a_pp{r}", [128, TT], F32))
                    for r in range(RC)]
            ps_p_sl = [Slot() for _ in range(RC)]
            def a_load(cg):
                return [u1s.load(u1[(cg * 2 + j) * 128:
                                    (cg * 2 + j + 1) * 128, :])
                        for j in range(2)]

            for t in range(2):
                emit_build_y(P, nc, ys, x_src, x_src_slot, t, yb, g, gsl)
                pend_a = a_load(0)
                for cg in range(DC // 2):
                    wis = pend_a
                    if cg + 1 < DC // 2:
                        pend_a = a_load(cg + 1)
                    for j in range(2):
                        cch = cg * 2 + j
                        for r in range(RC):
                            emit_group(P, ps_p[r][:, :], ps_p_sl[r],
                                       u1s.bufs(wis[j], r * 128, (r + 1) * 128),
                                       yb.aps(cch),
                                       u1s.slots(wis[j]), yb.slot(cch),
                                       first=(cch == 0), last=(cch == DC - 1))
                for r in range(RC):
                    dst = pslc(r, t)
                    P.op("scalar",
                         (lambda o, a: (lambda e: e.copy(out=o, in_=a)))(
                             dst[0], ps_p[r][:, :]),
                         reads=[ps_p_sl[r]], writes=[pb_sl[r * 2 + t]])
                    if split:
                        P.op("vector",
                             (lambda o, a, b: (lambda e: e.tensor_tensor(
                                 out=o, in0=a, in1=b, op=OP.subtract)))(
                                 dst[1], ps_p[r][:, :], dst[0]),
                             reads=[ps_p_sl[r]], writes=[pb_sl[r * 2 + t]])
            if DBG and next_stats:
                dbg_pf = nc.dram_tensor("dbg_pf", [128, RC * NT], F32,
                                        kind="ExternalOutput")
                P.store(dbg_pf.ap(), pb[0][:, :],
                        reads=[pb_sl[i] for i in range(RC * 2)])
            P.flush("ffn_a")

        with contextlib.ExitStack() as bctx:
            v1s = WStream(P, bctx, "b_v1", 64, 128, split=True, group=16)
            silu = [bctx.enter_context(SB(nc, f"b_si{i}", [128, TT], F32))
                    for i in range(2)]
            actf = [bctx.enter_context(SB(nc, f"b_af{i}", [128, TT], F32))
                    for i in range(2)]
            silu_sl = [Slot() for _ in range(2)]
            actf_sl = [P.rslot(bctx, kind="st") for _ in range(2)]
            if split:
                acth = [bctx.enter_context(SB(nc, f"b_ah{i}", [128, TT], BF16))
                        for i in range(2)]
                actl = [bctx.enter_context(SB(nc, f"b_al{i}", [128, TT], BF16))
                        for i in range(2)]
                acth_sl = [P.rslot(bctx, kind="st") for _ in range(2)]
                actl_sl = [P.rslot(bctx, kind="st") for _ in range(2)]
            ps_g = [bctx.enter_context(PS(nc, f"b_pg{i}", [128, TT], F32))
                    for i in range(4)]
            ps_u = [bctx.enter_context(PS(nc, f"b_pu{i}", [128, TT], F32))
                    for i in range(4)]
            ps_g_sl = [Slot() for _ in range(4)]
            ps_u_sl = [Slot() for _ in range(4)]
            def b_load(f):
                wg = [v1s.load(v1[r * 128:(r + 1) * 128, f * 128:(f + 1) * 128])
                      for r in range(RC)]
                wu = [v1s.load(v1[r * 128:(r + 1) * 128,
                                  (IC + f) * 128:(IC + f + 1) * 128])
                      for r in range(RC)]
                return wg, wu

            pend_b = b_load(0)
            for f in range(IC):
                wg, wu = pend_b
                if f + 1 < IC:
                    pend_b = b_load(f + 1)
                for t in range(2):
                    pbk = (f % 2) * 2 + t
                    for r in range(RC):
                        emit_group(P, ps_g[pbk][:, :], ps_g_sl[pbk],
                                   v1s.bufs(wg[r], 0, 128), pslc(r, t),
                                   v1s.slots(wg[r]), [pb_sl[r * 2 + t]],
                                   first=(r == 0), last=(r == RC - 1))
                        emit_group(P, ps_u[pbk][:, :], ps_u_sl[pbk],
                                   v1s.bufs(wu[r], 0, 128), pslc(r, t),
                                   v1s.slots(wu[r]), [pb_sl[r * 2 + t]],
                                   first=(r == 0), last=(r == RC - 1))
                    si = t % 2
                    P.op("scalar",
                         (lambda o, a, bias, pg: (lambda e: e.activation(
                             out=o, in_=a, func=AF.Silu, bias=bias, scale=1.0)))(
                             silu[si][:, :], ps_g[pbk][:, :], g["b1s"][:, f:f + 1],
                             pbk),
                         reads=[ps_g_sl[pbk], gsl["b1s"]], writes=[silu_sl[si]])
                    P.op("vector",
                         (lambda o, a, sc, b: (lambda e: e.scalar_tensor_tensor(
                             out=o, in0=a, scalar=sc, in1=b,
                             op0=OP.add, op1=OP.mult)))(
                             actf[si][:, :], ps_u[pbk][:, :],
                             g["b1s"][:, IC + f:IC + f + 1], silu[si][:, :]),
                         reads=[ps_u_sl[pbk], gsl["b1s"], silu_sl[si]],
                         writes=[actf_sl[si]])
                    if split:
                        P.op("scalar",
                             (lambda o, a: (lambda e: e.copy(out=o, in_=a)))(
                                 acth[si][:, :], actf[si][:, :]),
                             reads=[actf_sl[si]], writes=[acth_sl[si]])
                        P.op("vector",
                             (lambda o, a, b: (lambda e: e.tensor_tensor(
                                 out=o, in0=a, in1=b, op=OP.subtract)))(
                                 actl[si][:, :], actf[si][:, :], acth[si][:, :]),
                             reads=[actf_sl[si], acth_sl[si]], writes=[actl_sl[si]])
                        P.store(act_t.ap()[0, f, :, t * TT:(t + 1) * TT],
                                acth[si][:, :], reads=[acth_sl[si]],
                                writes=[act_slot])
                        P.store(act_t.ap()[1, f, :, t * TT:(t + 1) * TT],
                                actl[si][:, :], reads=[actl_sl[si]],
                                writes=[act_slot])
                    else:
                        P.store(act_t.ap()[f, :, t * TT:(t + 1) * TT],
                                actf[si][:, :], reads=[actf_sl[si]],
                                writes=[act_slot])
            P.flush("ffn_b")

        if split:
            zb = (fctx.enter_context(SB(nc, "zf_h", [128, RC * NT], BF16)),
                  fctx.enter_context(SB(nc, "zf_l", [128, RC * NT], BF16)))
        else:
            zb = (fctx.enter_context(SB(nc, "zf", [128, RC * NT], F32)),)
        zb_sl = [Slot() for _ in range(RC * 2)]

        def zslc(r, t):
            c0, c1 = r * NT + t * TT, r * NT + (t + 1) * TT
            return tuple(x[:, c0:c1] for x in zb)

        with contextlib.ExitStack() as cctx:
            u2s = WStream(P, cctx, "c_u2", 8, R, split=True, group=2)
            adt = F32 if PREC == "f32" else BF16
            acs = WStream(P, cctx, "c_ac", 8, TT, split=False, dtype=adt,
                          group=2)
            acl = (WStream(P, cctx, "c_al", 8, TT, split=False, dtype=BF16,
                           group=2)
                   if split else None)
            ps_z = [cctx.enter_context(PS(nc, f"c_pz{r}", [128, TT], F32))
                    for r in range(RC)]
            ps_z_sl = [Slot() for _ in range(RC)]
            for t in range(2):
                def c_load(fg, t=t):
                    fll = []
                    for j in range(2):
                        f = fg * 2 + j
                        wi = u2s.load(u2[f * 128:(f + 1) * 128, :])
                        if split:
                            ai = acs.load(act_t.ap()[0, f, :,
                                                     t * TT:(t + 1) * TT],
                                          src_slot=act_slot)
                            ali = acl.load(act_t.ap()[1, f, :,
                                                      t * TT:(t + 1) * TT],
                                           src_slot=act_slot)
                        else:
                            ai = acs.load(act_t.ap()[f, :, t * TT:(t + 1) * TT],
                                          src_slot=act_slot)
                            ali = None
                        fll.append((f, wi, ai, ali))
                    return fll

                pend_c = c_load(0)
                for fg in range(IC // 2):
                    fll = pend_c
                    if fg + 1 < IC // 2:
                        pend_c = c_load(fg + 1)
                    for f, wi, ai, ali in fll:
                        if split:
                            rhs = (acs.bufs(ai, 0, TT)[0],
                                   acl.bufs(ali, 0, TT)[0])
                            rsl = [acs.slots(ai)[0], acl.slots(ali)[0]]
                        else:
                            rhs = (acs.bufs(ai, 0, TT)[0],)
                            rsl = [acs.slots(ai)[0]]
                        for r in range(RC):
                            emit_group(P, ps_z[r][:, :], ps_z_sl[r],
                                       u2s.bufs(wi, r * 128, (r + 1) * 128),
                                       rhs, u2s.slots(wi), rsl,
                                       first=(f == 0), last=(f == IC - 1))
                for r in range(RC):
                    dst = zslc(r, t)
                    P.op("scalar",
                         (lambda o, a: (lambda e: e.copy(out=o, in_=a)))(
                             dst[0], ps_z[r][:, :]),
                         reads=[ps_z_sl[r]], writes=[zb_sl[r * 2 + t]])
                    if split:
                        P.op("vector",
                             (lambda o, a, b: (lambda e: e.tensor_tensor(
                                 out=o, in0=a, in1=b, op=OP.subtract)))(
                                 dst[1], ps_z[r][:, :], dst[0]),
                             reads=[ps_z_sl[r]], writes=[zb_sl[r * 2 + t]])
                if DBG and next_stats and t == 1:
                    dbg_zf = nc.dram_tensor("dbg_zf", [128, RC * NT], F32,
                                            kind="ExternalOutput")
                    P.store(dbg_zf.ap(), zb[0][:, :],
                            reads=[zb_sl[i] for i in range(RC * 2)])
                P.flush(f"ffn_c{t}")

        if next_stats:
            ps_st = [fctx.enter_context(PS(nc, f"d_ps{i}", [1, TT], F32))
                     for i in range(4)]
            ps_st_sl = [Slot() for _ in range(4)]
        with contextlib.ExitStack() as dctx:
            v2s = WStream(P, dctx, "d_v2", 16, TT, split=True, group=4)
            xc2 = [dctx.enter_context(SB(nc, f"d_xc{i}", [128, TT], F32))
                   for i in range(2)]
            xo = [dctx.enter_context(SB(nc, f"d_xo{i}", [128, TT], F32))
                  for i in range(2)]
            tb = [dctx.enter_context(SB(nc, f"d_tb{i}", [128, TT], F32))
                  for i in range(2)]
            xc2_sl = [P.rslot(dctx, waw=False) for _ in range(2)]
            xo_sl = [P.rslot(dctx, kind="st") for _ in range(2)]
            tb_sl = [Slot() for _ in range(2)]
            ps_o = [dctx.enter_context(PS(nc, f"d_po{i}", [128, TT], F32))
                    for i in range(4)]
            ps_o_sl = [Slot() for _ in range(4)]
            dseq = [(dog, rg) for dog in range(8) for rg in range(2)]

            def d_load(idx, t):
                dog, rg = dseq[idx]
                return [v2s.load(v2[(rg * 4 + k) * 128:
                                    (rg * 4 + k + 1) * 128,
                                    dog * 512:(dog + 1) * 512])
                        for k in range(4)]

            for t in range(2):
                pend_d = d_load(0, t)
                for idx, (dog, rg) in enumerate(dseq):
                    wis = pend_d
                    if idx + 1 < len(dseq):
                        pend_d = d_load(idx + 1, t)
                    for k in range(4):
                        r = rg * 4 + k
                        for do in range(4):
                            emit_group(P, ps_o[do][:, :], ps_o_sl[do],
                                       v2s.bufs(wis[k], do * 128,
                                                (do + 1) * 128),
                                       zslc(r, t),
                                       v2s.slots(wis[k]),
                                       [zb_sl[r * 2 + t]],
                                       first=(r == 0), last=(r == RC - 1))
                    if rg != 1:
                        continue
                    for do in range(4):
                        dd = dog * 4 + do
                        xs = dd % 2
                        P.load(xc2[xs][:, :],
                               x_src[dd * 128:(dd + 1) * 128, t * TT:(t + 1) * TT],
                               reads=[x_src_slot], writes=[xc2_sl[xs]])
                        P.op("scalar",
                             (lambda o, a, bias: (lambda e: e.activation(
                                 out=o, in_=a, func=AF.Identity,
                                 bias=bias, scale=0.5)))(
                                 tb[xs][:, :], ps_o[do][:, :], g["b2h"][:, dd:dd + 1]),
                             reads=[ps_o_sl[do], gsl["b2h"]], writes=[tb_sl[xs]])
                        P.op("vector",
                             (lambda o, a, b: (lambda e: e.tensor_tensor(
                                 out=o, in0=a, in1=b, op=OP.add)))(
                                 xo[xs][:, :], tb[xs][:, :], xc2[xs][:, :]),
                             reads=[tb_sl[xs], xc2_sl[xs]], writes=[xo_sl[xs]])
                        P.store(x_dst[dd * 128:(dd + 1) * 128, t * TT:(t + 1) * TT],
                                xo[xs][:, :], reads=[xo_sl[xs]],
                                writes=[x_dst_slot])
                        if next_stats:
                            P.op("vector",
                                 (lambda o, a: (lambda e: e.tensor_tensor(
                                     out=o, in0=a, in1=a, op=OP.mult)))(
                                     tb[xs][:, :], xo[xs][:, :]),
                                 reads=[xo_sl[xs]], writes=[tb_sl[xs]])
                            first = (dog == 0 and do == 0)
                            last = (dog == 7 and do == 3)
                            emit_group(P, ps_st[2 * t][:, :], ps_st_sl[2 * t],
                                       (g["ones_kcol"][:, :],), (xo[xs][:, :],),
                                       [gsl["ones_kcol"]], [xo_sl[xs]],
                                       first=first, last=last)
                            emit_group(P, ps_st[2 * t + 1][:, :], ps_st_sl[2 * t + 1],
                                       (g["ones_kcol"][:, :],), (tb[xs][:, :],),
                                       [gsl["ones_kcol"]], [tb_sl[xs]],
                                       first=first, last=last)
            P.flush("ffn_d")
        if next_stats:
            with PS(nc, "d_bc", [128, NT], F32) as ps_b:
                finish_stats(P, [ps_st[0], ps_st[2]], [ps_st_sl[0], ps_st_sl[2]],
                             [ps_st[1], ps_st[3]], [ps_st_sl[1], ps_st_sl[3]],
                             ps_b, g, gsl)
                P.flush("ffn_dst")



def emit_qkv(P, nc, x1, x1_slot, wq, wk, wv, q_ap, q_slot, kvs, kvs_slot,
             kvf, kvf_slots, g, gsl):
    split = PREC == "bf16x3"

    def ag(src_off, n, dst_off, dst_slot):
        P.collective(
            (lambda so, nn, do: (lambda e: e.collective_compute(
                "AllGather", OP.bypass,
                replica_groups=[list(range(N_CORES))],
                ins=[bass.AP(kvs, so, [[1, nn]]).opt()],
                outs=[bass.AP(kvf, do, [[1, N_CORES * nn]]).opt()])))(
                    src_off, n, dst_off),
            reads=[kvs_slot], writes=[dst_slot])
    with contextlib.ExitStack() as ectx:
        yb = YBufs(P, ectx, nc, "e_y")
        ys = YScratch(P, ectx, nc, "e_ys")
        ws = WStream(P, ectx, "e_w", 16, TT, split=True, group=4)
        qout = [ectx.enter_context(SB(nc, f"e_qo{i}", [128, TT], F32))
                for i in range(2)]
        qout_sl = [P.rslot(ectx, kind="st") for _ in range(2)]
        for t in range(2):
            emit_build_y(P, nc, ys, x1, x1_slot, t, yb, g, gsl)
            with contextlib.ExitStack() as qctx:
                ps_q = [qctx.enter_context(PS(nc, f"e_pq{t}_{i}", [128, TT], F32))
                        for i in range(4)]
                ps_q_sl = [Slot() for _ in range(4)]
                qkseq = [(which, wmat, og, cg)
                         for which, wmat, nout in (("q", wq, DC), ("k", wk, RC))
                         for og in range(nout // 4)
                         for cg in range(DC // 4)]

                def qk_load(idx):
                    _, wmat_, og_, cg_ = qkseq[idx]
                    return [ws.load(wmat_[(cg_ * 4 + k) * 128:
                                          (cg_ * 4 + k + 1) * 128,
                                          og_ * 512:(og_ + 1) * 512])
                            for k in range(4)]

                pend_q = qk_load(0)
                for idx, (which, wmat, og, cg) in enumerate(qkseq):
                    wis = pend_q
                    if idx + 1 < len(qkseq):
                        pend_q = qk_load(idx + 1)
                    for k in range(4):
                        cch = cg * 4 + k
                        for j in range(4):
                            emit_group(P, ps_q[j][:, :], ps_q_sl[j],
                                       ws.bufs(wis[k], j * 128,
                                               (j + 1) * 128),
                                       yb.aps(cch),
                                       ws.slots(wis[k]), yb.slot(cch),
                                       first=(cch == 0),
                                       last=(cch == DC - 1))
                    if cg != DC // 4 - 1:
                        continue
                    for j in range(4):
                            oc = og * 4 + j
                            qs_ = oc % 2
                            P.op("scalar",
                                 (lambda o, a: (lambda e: e.copy(out=o, in_=a)))(
                                     qout[qs_][:, :], ps_q[j][:, :]),
                                 reads=[ps_q_sl[j]], writes=[qout_sl[qs_]])
                            if which == "q":
                                dst = q_ap[oc * 128:(oc + 1) * 128,
                                           t * TT:(t + 1) * TT]
                                dslot = q_slot
                            else:
                                dst = bass.AP(kvs, oc * 128 * NT + t * TT,
                                              [[NT, 128], [1, TT]])
                                dslot = kvs_slot
                            P.store(dst, qout[qs_][:, :],
                                    reads=[qout_sl[qs_]], writes=[dslot])
                P.flush(f"qkv_qk{t}")
                if t == 1:
                    ag(0, RN, 0, kvf_slots[0])

            with contextlib.ExitStack() as vctx:
                ps_v = [vctx.enter_context(PS(nc, f"e_pv{t}_{i}", [128, TT], F32))
                        for i in range(8)]
                ps_v_sl = [Slot() for _ in range(8)]
                vo = [vctx.enter_context(SB(nc, f"e_vo{t}_{i}", [128, R], F32))
                      for i in range(2)]
                vo_sl = [P.rslot(vctx, kind="st") for _ in range(2)]
                def v_load(cg):
                    pairs = []
                    for j in range(2):
                        cch = cg * 2 + j
                        wi = ws.load(wv[cch * 128:(cch + 1) * 128, 0:512])
                        wi2 = ws.load(wv[cch * 128:(cch + 1) * 128, 512:1024])
                        pairs.append((cch, wi, wi2))
                    return pairs

                pend_v = v_load(0)
                for cg in range(DC // 2):
                    pairs = pend_v
                    if cg + 1 < DC // 2:
                        pend_v = v_load(cg + 1)
                    for cch, wi, wi2 in pairs:
                        for ts in range(4):
                            for dvh, wii in ((0, wi), (1, wi2)):
                                emit_group(P, ps_v[ts * 2 + dvh][:, :],
                                           ps_v_sl[ts * 2 + dvh],
                                           yb.aps(cch, ts * 128, (ts + 1) * 128),
                                           ws.bufs(wii, 0, 512),
                                           yb.slot(cch), ws.slots(wii),
                                           first=(cch == 0),
                                           last=(cch == DC - 1))
                for ts in range(4):
                    vi = ts % 2
                    for dvh in range(2):
                        P.op("vector",
                             (lambda o, a, b: (lambda e: e.tensor_tensor(
                                 out=o, in0=a, in1=b, op=OP.add)))(
                                 vo[vi][:, dvh * 512:(dvh + 1) * 512],
                                 ps_v[ts * 2 + dvh][:, :],
                                 g["bvb"][:, dvh * 512:(dvh + 1) * 512]),
                             reads=[ps_v_sl[ts * 2 + dvh], gsl["bvb"]],
                             writes=[vo_sl[vi]])
                    P.store(bass.AP(kvs, RN + (t * TT + ts * 128) * R,
                                    [[R, 128], [1, R]]),
                            vo[vi][:, :], reads=[vo_sl[vi]], writes=[kvs_slot])
                P.flush(f"qkv_v{t}")
                if t == 0:
                    ag(RN, TT * R, 8 * RN, kvf_slots[1])
                else:
                    ag(RN + TT * R, TT * R, 12 * RN,
                       kvf_slots[2])



def emit_attention(P, nc, kvf, kvf_slots, q_ap, q_slot, wo, x1, x1_slot,
                   x2, x2_slot, g, gsl):
    split = PREC == "bf16x3"
    with contextlib.ExitStack() as gctx:
        def sb(name, shape, dt=F32):
            return gctx.enter_context(SB(nc, name, shape, dt))

        assert split
        if split:
            ctxb = (sb("g_ctxh", [128, DC * TT], BF16),
                    sb("g_ctxl", [128, DC * TT], BF16))
        else:
            ctxb = (sb("g_ctxf", [128, DC * TT]),)
        ctx_sl = [Slot() for _ in range(DC)]

        ps_st = [gctx.enter_context(PS(nc, f"g_ps{i}", [1, TT], F32))
                 for i in range(4)]
        ps_st_sl = [Slot() for _ in range(4)]

        def ctxslc(c):
            c0, c1 = c * TT, (c + 1) * TT
            return tuple(x[:, c0:c1] for x in ctxb)

        for b in range(2):
            with contextlib.ExitStack() as actx:
                ps_s = [actx.enter_context(PS(nc, f"g_pss{b}_{i}",
                                              [128, 2 * QS], F32))
                        for i in range(2)]
                ps_s_sl = [Slot() for _ in range(2)]
                ps_c = actx.enter_context(PS(nc, f"g_psc{b}", [128, QS], F32))
                ps_c_sl = Slot()
                ps_bc = actx.enter_context(PS(nc, f"g_psbc{b}", [128, QS], F32))
                ps_bc_sl = Slot()
                ps_sro = ps_bc[0:1, :]
                ps_sro_sl = ps_bc_sl

                def asb(name, shape, dt=F32):
                    return actx.enter_context(SB(nc, name, shape, dt))

                kraw = asb(f"g_kraw{b}", [128, S_KEYS // 2])
                kraw_sl = P.rslot(actx, waw=False)
                kh = [asb(f"g_kh{b}_{i}", [128, S_KEYS], BF16) for i in range(2)]
                kl = [asb(f"g_kl{b}_{i}", [128, S_KEYS], BF16) for i in range(2)]
                kh_sl = [Slot() for _ in range(2)]
                kl_sl = [Slot() for _ in range(2)]
                vbuf = [asb(f"g_vbuf{b}_{i}", [128, S_KEYS]) for i in range(2)]
                vbuf_sl = [P.rslot(actx, waw=False) for _ in range(2)]
                qraw = asb(f"g_qraw{b}", [128, TT])
                qraw_sl = P.rslot(actx, waw=False)
                qh = [asb(f"g_qh{b}_{i}", [128, TT], BF16) for i in range(2)]
                ql = [asb(f"g_ql{b}_{i}", [128, TT], BF16) for i in range(2)]
                qh_sl = [Slot() for _ in range(2)]
                ql_sl = [Slot() for _ in range(2)]
                ssb = asb(f"g_ssb{b}", [128, KC * QS])
                ssb_sl = [Slot() for _ in range(KC)]
                sumb = asb(f"g_sumb{b}", [128, QS])
                sumb_sl = Slot()
                invb = asb(f"g_invb{b}", [128, QS])
                invb_sl = Slot()
                cbuf = asb(f"g_cbuf{b}", [128, QS])
                cbuf_sl = Slot()

                def load_kv(h):
                    hb = h % 2
                    HS = S_KEYS // 2
                    for half in range(2):
                        for si in range(4):
                            s = 4 * half + si
                            src = bass.AP(kvf,
                                          s * RN + h * 128 * NT + b * TT,
                                          [[NT, 128], [1, TT]])
                            P.load(kraw[:, si * TT:(si + 1) * TT], src,
                                   reads=[kvf_slots[0]], writes=[kraw_sl])
                        P.op("scalar",
                             (lambda hb_, hf: (lambda e: e.copy(
                                 out=kh[hb_][:, hf * HS:(hf + 1) * HS],
                                 in_=kraw[:, :])))(hb, half),
                             reads=[kraw_sl], writes=[kh_sl[hb]])
                        P.op("vector",
                             (lambda hb_, hf: (lambda e: e.tensor_tensor(
                                 out=kl[hb_][:, hf * HS:(hf + 1) * HS],
                                 in0=kraw[:, :],
                                 in1=kh[hb_][:, hf * HS:(hf + 1) * HS],
                                 op=OP.subtract)))(hb, half),
                             reads=[kraw_sl, kh_sl[hb]], writes=[kl_sl[hb]])
                    for j in range(KC):
                        s, jl = j // 4, j % 4
                        src = bass.AP(kvf,
                                      (8 + 4 * b) * RN + s * (TT * R)
                                      + (jl * 128) * R + h * 128,
                                      [[R, 128], [1, 128]])
                        P.load(vbuf[hb][:, j * 128:(j + 1) * 128], src,
                               reads=[kvf_slots[1 + b]],
                               writes=[vbuf_sl[hb]])

                def load_q(qhn, gb):
                    P.load(qraw[:, :],
                           q_ap[qhn * 128:(qhn + 1) * 128, b * TT:(b + 1) * TT],
                           reads=[q_slot], writes=[qraw_sl])
                    P.op("scalar",
                         (lambda gb_: (lambda e: e.copy(
                             out=qh[gb_][:, :], in_=qraw[:, :])))(gb),
                         reads=[qraw_sl], writes=[qh_sl[gb]])
                    P.op("vector",
                         (lambda gb_: (lambda e: e.tensor_tensor(
                             out=ql[gb_][:, :], in0=qraw[:, :],
                             in1=qh[gb_][:, :], op=OP.subtract)))(gb),
                         reads=[qraw_sl, qh_sl[gb]], writes=[ql_sl[gb]])

                load_kv(0)
                for h in range(NKV):
                    hb = h % 2
                    load_q(4 * h, 0)
                    if h + 1 < NKV:
                        load_kv(h + 1)
                    k_aps = (kh[hb], kl[hb])
                    k_sls = [kh_sl[hb], kl_sl[hb]]
                    for gq in range(GQ):
                        qhn = 4 * h + gq
                        gb = gq % 2
                        if gq + 1 < GQ:
                            load_q(qhn + 1, (gq + 1) % 2)
                        q_aps = (qh[gb], ql[gb])
                        q_sls = [qh_sl[gb], ql_sl[gb]]
                        if ATTN_LVL < 1:
                            continue
                        for qs in range(2):
                            qc = slice(qs * QS, (qs + 1) * QS)
                            for kp in range(KC // 2):
                                pbk = kp % 2
                                for half in range(2):
                                    kc = 2 * kp + half
                                    lhs = tuple(a[:, kc * 128:(kc + 1) * 128]
                                                for a in k_aps)
                                    rhs = tuple(a[:, qc] for a in q_aps)
                                    emit_group(P,
                                               ps_s[pbk][:, half * QS:
                                                         (half + 1) * QS],
                                               ps_s_sl[pbk],
                                               lhs, rhs, k_sls, q_sls,
                                               first=True, last=True)
                                if ATTN_LVL < 2:
                                    continue
                                P.op("scalar",
                                     (lambda o, a: (lambda e: e.activation(
                                         out=o, in_=a, func=AF.Exp,
                                         bias=0.0, scale=ATT_SCALE)))(
                                         ssb[:, kp * 2 * QS:(kp + 1) * 2 * QS],
                                         ps_s[pbk][:, :]),
                                     reads=[ps_s_sl[pbk]], writes=[ssb_sl[kp]])
                                if ATTN_LVL < 3:
                                    continue
                                if kp == 0:
                                    P.op("vector",
                                         lambda e: e.tensor_tensor(
                                             out=sumb[:, :], in0=ssb[:, 0:QS],
                                             in1=ssb[:, QS:2 * QS], op=OP.add),
                                         reads=[ssb_sl[0]], writes=[sumb_sl])
                                else:
                                    for half in range(2):
                                        kc = 2 * kp + half
                                        P.op("vector",
                                             (lambda a: (lambda e: e.tensor_tensor(
                                                 out=sumb[:, :], in0=sumb[:, :],
                                                 in1=a, op=OP.add)))(
                                                 ssb[:, kc * QS:(kc + 1) * QS]),
                                             reads=[ssb_sl[kp], sumb_sl],
                                             writes=[sumb_sl])
                            if ATTN_LVL < 5:
                                continue
                            for kc in range(KC - 1, -1, -1):
                                emit_group(P, ps_c[:, :], ps_c_sl,
                                           (vbuf[hb][:, kc * 128:(kc + 1) * 128],),
                                           (ssb[:, kc * QS:(kc + 1) * QS],),
                                           [vbuf_sl[hb]], [ssb_sl[kc // 2]],
                                           first=(kc == KC - 1), last=(kc == 0))
                            P.op("tensor",
                                 lambda e: e.matmul(ps_sro, g["ones_kcol"][:, :],
                                                    sumb[:, :], start=True, stop=True),
                                 reads=[sumb_sl, gsl["ones_kcol"]], writes=[ps_sro_sl])
                            P.op("vector",
                                 lambda e: e.reciprocal(invb[0:1, :], ps_sro),
                                 reads=[ps_sro_sl], writes=[invb_sl])
                            P.op("tensor",
                                 lambda e: e.matmul(ps_bc[:, :], g["ones_brow"][:, :],
                                                    invb[0:1, :], start=True, stop=True),
                                 reads=[invb_sl, gsl["ones_brow"]], writes=[ps_bc_sl])
                            P.op("scalar",
                                 lambda e: e.copy(out=invb[:, :], in_=ps_bc[:, :]),
                                 reads=[ps_bc_sl], writes=[invb_sl])
                            P.op("vector",
                                 lambda e: e.tensor_tensor(
                                     out=cbuf[:, :], in0=ps_c[:, :],
                                     in1=invb[:, :], op=OP.mult),
                                 reads=[ps_c_sl, invb_sl], writes=[cbuf_sl])
                            dst0 = qhn * TT + qs * QS
                            P.op("scalar",
                                 (lambda o: (lambda e: e.copy(out=o, in_=cbuf[:, :])))(
                                     ctxb[0][:, dst0:dst0 + QS]),
                                 reads=[cbuf_sl], writes=[ctx_sl[qhn]])
                            if split:
                                P.op("vector",
                                     (lambda o, hh: (lambda e: e.tensor_tensor(
                                         out=o, in0=cbuf[:, :], in1=hh,
                                         op=OP.subtract)))(
                                         ctxb[1][:, dst0:dst0 + QS],
                                         ctxb[0][:, dst0:dst0 + QS]),
                                     reads=[cbuf_sl], writes=[ctx_sl[qhn]])
                P.flush(f"attn{b}")

            if ATTN_LVL < 6:
                continue
            with contextlib.ExitStack() as wctx:
                wos = WStream(P, wctx, "g_wo", 10, TT, split=True, group=2)
                x1c = [wctx.enter_context(SB(nc, f"g_x1c{b}_{i}", [128, TT], F32))
                       for i in range(2)]
                x2o = [wctx.enter_context(SB(nc, f"g_x2o{b}_{i}", [128, TT], F32))
                       for i in range(2)]
                x2q = [wctx.enter_context(SB(nc, f"g_x2q{b}_{i}", [128, TT], F32))
                       for i in range(2)]
                x1c_sl = [P.rslot(wctx, waw=False) for _ in range(2)]
                x2o_sl = [P.rslot(wctx, kind="st") for _ in range(2)]
                x2q_sl = [Slot() for _ in range(2)]
                ps_w = [wctx.enter_context(PS(nc, f"g_pw{b}_{i}", [128, TT], F32))
                        for i in range(4)]
                ps_w_sl = [Slot() for _ in range(4)]
                woseq = [(dog, cg) for dog in range(8)
                         for cg in range(DC // 2)]

                def wo_load(idx):
                    dog_, cg_ = woseq[idx]
                    return [wos.load(wo[(cg_ * 2 + k) * 128:
                                        (cg_ * 2 + k + 1) * 128,
                                        dog_ * 512:(dog_ + 1) * 512])
                            for k in range(2)]

                pend_w = wo_load(0)
                for idx, (dog, cg) in enumerate(woseq):
                    wis = pend_w
                    if idx + 1 < len(woseq):
                        pend_w = wo_load(idx + 1)
                    for k in range(2):
                        cch = cg * 2 + k
                        for do in range(4):
                            emit_group(P, ps_w[do][:, :], ps_w_sl[do],
                                       wos.bufs(wis[k], do * 128,
                                                (do + 1) * 128),
                                       ctxslc(cch),
                                       wos.slots(wis[k]), [ctx_sl[cch]],
                                       first=(cch == 0),
                                       last=(cch == DC - 1))
                    if cg != DC // 2 - 1:
                        continue
                    for do in range(4):
                        dd = dog * 4 + do
                        xs = dd % 2
                        P.load(x1c[xs][:, :],
                               x1[dd * 128:(dd + 1) * 128, b * TT:(b + 1) * TT],
                               reads=[x1_slot], writes=[x1c_sl[xs]])
                        P.op("vector",
                             (lambda o, a, bb: (lambda e: e.tensor_tensor(
                                 out=o, in0=a, in1=bb, op=OP.add)))(
                                 x2o[xs][:, :], ps_w[do][:, :], x1c[xs][:, :]),
                             reads=[ps_w_sl[do], x1c_sl[xs]], writes=[x2o_sl[xs]])
                        P.store(x2[dd * 128:(dd + 1) * 128, b * TT:(b + 1) * TT],
                                x2o[xs][:, :], reads=[x2o_sl[xs]], writes=[x2_slot])
                        P.op("vector",
                             (lambda o, a: (lambda e: e.tensor_tensor(
                                 out=o, in0=a, in1=a, op=OP.mult)))(
                                 x2q[xs][:, :], x2o[xs][:, :]),
                             reads=[x2o_sl[xs]], writes=[x2q_sl[xs]])
                        first = (dog == 0 and do == 0)
                        last = (dog == 7 and do == 3)
                        emit_group(P, ps_st[2 * b][:, :], ps_st_sl[2 * b],
                                   (g["ones_kcol"][:, :],), (x2o[xs][:, :],),
                                   [gsl["ones_kcol"]], [x2o_sl[xs]],
                                   first=first, last=last)
                        emit_group(P, ps_st[2 * b + 1][:, :], ps_st_sl[2 * b + 1],
                                   (g["ones_kcol"][:, :],), (x2q[xs][:, :],),
                                   [gsl["ones_kcol"]], [x2q_sl[xs]],
                                   first=first, last=last)
                P.flush(f"wo{b}")
        with PS(nc, "g_bc", [128, NT], F32) as ps_b:
            finish_stats(P, [ps_st[0], ps_st[2]], [ps_st_sl[0], ps_st_sl[2]],
                         [ps_st[1], ps_st[3]], [ps_st_sl[1], ps_st_sl[3]],
                         ps_b, g, gsl)
            P.flush("attn_st")



def _prep_inputs(inputs):
    x = np.asarray(inputs["L0_attn_hidden_states"], dtype=np.float32)
    wq = np.ascontiguousarray(np.asarray(inputs["L0_attn_Wq"], np.float32).T)
    wk = np.ascontiguousarray(np.asarray(inputs["L0_attn_Wk"], np.float32).T)
    wv = np.ascontiguousarray(np.asarray(inputs["L0_attn_Wv"], np.float32).T)
    wo = np.ascontiguousarray(np.asarray(inputs["L0_attn_Wo"], np.float32).T)
    bv = np.asarray(inputs["L0_attn_bv"], np.float32)
    u1 = np.ascontiguousarray(np.asarray(inputs["L0_ffn_U1"], np.float32))
    v1 = np.ascontiguousarray(np.asarray(inputs["L0_ffn_V1"], np.float32))
    u2 = np.ascontiguousarray(np.asarray(inputs["L0_ffn_U2"], np.float32))
    v2 = np.ascontiguousarray(np.asarray(inputs["L0_ffn_V2"], np.float32))
    b1 = np.asarray(inputs["L0_ffn_b1"], np.float32)
    b2 = np.asarray(inputs["L0_ffn_b2"], np.float32)

    b1r = np.ascontiguousarray(b1.reshape(2 * IC, 128).T)
    b2r = np.ascontiguousarray(b2.reshape(DC, 128).T)
    bvb = np.ascontiguousarray(np.broadcast_to(bv[None, :], (128, R)))

    shared = dict(u1=u1, v1=v1, u2=u2, v2=v2, b1r=b1r, b2r=b2r, bvb=bvb,
                  wq=wq, wk=wk, wv=wv, wo=wo)
    in_maps = []
    for c in range(N_CORES):
        xc = np.concatenate(
            [x[0, c * TT:(c + 1) * TT, :], x[1, c * TT:(c + 1) * TT, :]], axis=0)
        xc = np.ascontiguousarray(xc.T)
        m = dict(shared)
        m["x"] = xc
        in_maps.append(m)
    return in_maps


def kernel(**inputs):
    in_maps = _prep_inputs(inputs)
    nc = build_nc()
    res = run_bass_kernel_spmd(nc, in_maps, core_ids=list(range(N_CORES)))
    out = np.empty((2, N_CORES * TT, D), dtype=np.float32)
    for c in range(N_CORES):
        oc = res.results[c]["out"]
        out[0, c * TT:(c + 1) * TT, :] = oc[:, 0:TT].T
        out[1, c * TT:(c + 1) * TT, :] = oc[:, TT:NT].T
    return out
